# version 29
# baseline (speedup 1.0000x reference)
import numpy as np
from contextlib import ExitStack

import concourse.bass as bass
import concourse.mybir as mybir
from concourse.bass_utils import run_bass_kernel_spmd

F32 = mybir.dt.float32
BF16 = mybir.dt.bfloat16
AF = mybir.ActivationFunctionType
ALU = mybir.AluOpType
AX = mybir.AxisListType

D = 1024
NF = 8
H = 16
DH = 64
NPAIR = 8
DFF = 4096
NQ = 32
EPS_LN = 1e-5
EPS_ATTN = 1e-6
CV = float(2 ** 20 + 1)
N_CORES = 8
PM2 = mybir.MatmulPerfMode.DoubleRow


def build(L):
    NT = L // 128
    NTT = L // 512
    nc = bass.Bass("TRN2", target_bir_lowering=False)

    dx = nc.dram_tensor("x", [L, D], F32, kind="ExternalInput").ap()
    dws = {n: nc.dram_tensor(n, [D, D], F32, kind="ExternalInput").ap()
           for n in ("wq", "wk", "wv", "wo", "wf")}
    dw1 = nc.dram_tensor("w1", [D, DFF], F32, kind="ExternalInput").ap()
    dw2 = nc.dram_tensor("w2", [DFF, D], F32, kind="ExternalInput").ap()
    dvec = {n: nc.dram_tensor(n, [D], F32, kind="ExternalInput").ap()
            for n in ("bq", "bk", "bv", "bo", "b2", "fb", "g1", "be1", "g2", "be2")}
    db1 = nc.dram_tensor("b1", [DFF], F32, kind="ExternalInput").ap()
    deyeb = nc.dram_tensor("eyeb", [128, 128], BF16, kind="ExternalInput").ap()
    deyef = nc.dram_tensor("eyef", [128, 128], F32, kind="ExternalInput").ap()
    donescb = nc.dram_tensor("onescb", [128, 1], BF16, kind="ExternalInput").ap()
    donesrf = nc.dram_tensor("onesrf", [1, 128], F32, kind="ExternalInput").ap()
    donesrb = nc.dram_tensor("onesrb", [1, 128], BF16, kind="ExternalInput").ap()
    depsln = nc.dram_tensor("epsln", [128, 1], F32, kind="ExternalInput").ap()
    dy = nc.dram_tensor("y", [L, D], F32, kind="ExternalOutput").ap()
    cc_in = nc.dram_tensor("cc_in", [1, 512], F32, kind="Internal").ap()
    cc_out = nc.dram_tensor("cc_out", [1, 512], F32, kind="Internal").ap()

    xr = dx.rearrange("(c p) n -> c p n", p=128)
    yr = dy.rearrange("(c p) n -> c p n", p=128)
    wr = {n: a.rearrange("(k p) n -> k p n", p=128) for n, a in dws.items()}
    w1r = dw1.rearrange("(k p) n -> k p n", p=128)
    w2r = dw2.rearrange("(k p) n -> k p n", p=128)

    t = nc.tensor
    v = nc.vector
    s = nc.scalar
    g = nc.gpsimd
    sy = nc.sync

    cnt = {}

    def token(inst, sem, k=1):
        inst.then_inc(sem, k)
        cnt[sem.name] = cnt.get(sem.name, 0) + k
        return (sem, cnt[sem.name])

    last_wait = {}

    def wait(eng, tok):
        if tok is None:
            return
        if isinstance(tok, list):
            for tk in tok:
                wait(eng, tk)
            return
        sem, val = tok
        key = (id(eng), sem.name)
        if last_wait.get(key, -1) >= val:
            return
        eng.wait_ge(sem, val)
        last_wait[key] = val

    def now(sem):
        return (sem, cnt.get(sem.name, 0))

    ST = ExitStack()
    with ST:
        E = ST.enter_context
        sCON = E(nc.semaphore("sCON"))
        sWA = E(nc.semaphore("sWA"))
        sWB = E(nc.semaphore("sWB"))
        sX = [E(nc.semaphore(f"sX{i}")) for i in range(2)]
        sW1 = [E(nc.semaphore(f"sW1_{i}")) for i in range(2)]
        sW2 = [E(nc.semaphore(f"sW2_{i}")) for i in range(5)]
        sROW = E(nc.semaphore("sROW"))
        sOUT = [E(nc.semaphore(f"sOUT{i}")) for i in range(2)]
        sCC = E(nc.semaphore("sCC"))
        sMISC = E(nc.semaphore("sMISC"))
        sWQ = [E(nc.semaphore(f"sWQ{i}")) for i in range(2)]
        sCCP = E(nc.semaphore("sCCP"))
        ALL_DMA = [sCON, sWA, sWB, sROW, sCC, sMISC] + sX + sW1 + sW2 + sOUT + sWQ
        sT = E(nc.semaphore("sT"))
        sV = E(nc.semaphore("sV"))
        sS = E(nc.semaphore("sS"))
        sP = E(nc.semaphore("sP"))

        def barrier():
            toks = [token(t.drain(), sT), token(v.drain(), sV),
                    token(s.drain(), sS), token(g.drain(), sP)]
            toks += [now(x) for x in ALL_DMA]
            for e in (t, v, s, g, sy):
                for tk in toks:
                    wait(e, tk)

        eyeb = E(nc.sbuf_tensor("eyeb_sb", [128, 128], BF16))
        eyef = E(nc.sbuf_tensor("eyef_sb", [128, 128], F32))
        onescb = E(nc.sbuf_tensor("onescb_sb", [128, 1], BF16))
        onesrf = E(nc.sbuf_tensor("onesrf_sb", [1, 128], F32))
        onesrb = E(nc.sbuf_tensor("onesrb_sb", [1, 128], BF16))
        bq_c = E(nc.sbuf_tensor("bq_c", [128, NF], F32))
        nbq_c = E(nc.sbuf_tensor("nbq_c", [128, NF], F32))
        b1_c = E(nc.sbuf_tensor("b1_c", [128, NQ], F32))
        pool = [E(nc.sbuf_tensor(f"pool{i}", [128, D], F32)) for i in range(4)]
        mu = {i: E(nc.sbuf_tensor(f"mu{i}", [128, NT], F32)) for i in range(4)}
        rr = {i: E(nc.sbuf_tensor(f"rr{i}", [128, NT], F32)) for i in range(4)}
        rs3 = E(nc.sbuf_tensor("rs3", [128, NT], F32))
        sc_sts = {i: E(nc.sbuf_tensor(f"sc_st{i}", [128, NT], F32)) for i in range(4)}
        sq_sts = {i: E(nc.sbuf_tensor(f"sq_st{i}", [128, NT], F32)) for i in range(4)}
        am_c = E(nc.sbuf_tensor("am_c", [128, NT], F32))
        smalls = E(nc.sbuf_tensor("smalls", [128, 8], F32))
        row128 = E(nc.sbuf_tensor("row128", [1, 128], F32))
        sc1 = E(nc.sbuf_tensor("sc1", [1, 8], F32))
        pad = E(nc.sbuf_tensor("pad", [1, 512], F32))
        sxB = E(nc.sbuf_tensor("sxB", [128, 2], F32))
        xslot = [E(nc.sbuf_tensor(f"xslot{i}", [128, D], F32)) for i in range(2)]
        rowf = E(nc.sbuf_tensor("rowf", [1, D], F32))
        epsln = E(nc.sbuf_tensor("epsln_sb", [128, 1], F32))
        b2_row = E(nc.sbuf_tensor("b2_row", [1, D], BF16))

        psB = [E(nc.psum_tensor(f"psB{i}", [128, 512], F32)) for i in range(4)]

        for dst, src in ((eyeb, deyeb), (eyef, deyef), (onescb, donescb)):
            token(g.dma_start(out=dst[:], in_=src[:, :]), sCON, 16)
        token(g.dma_start(out=onesrf[:1, :], in_=donesrf[:, :]), sCON, 16)
        token(g.dma_start(out=onesrb[:1, :], in_=donesrb[:, :]), sCON, 16)
        token(g.dma_start(out=epsln[:], in_=depsln[:, :]), sCON, 16)
        consts_tok = now(sCON)

        ones11 = onesrf[0:1, 0:1]
        rowf_free = None
        colld_prev = None

        def col_load(dst_cols, vec_ap, n_cols):
            nonlocal rowf_free, colld_prev
            etk = None
            for r in range((n_cols + 7) // 8):
                wait(sy, rowf_free)
                rtk = token(sy.dma_start(out=rowf[:1, :],
                                         in_=vec_ap[r * D:(r + 1) * D][None, :]), sROW, 16)
                wait(t, rtk)
                wait(t, consts_tok)
                wait(t, colld_prev)
                nb = min(8, n_cols - r * 8)
                for f in range(nb):
                    mm = t.matmul(psB[0][:, f:f + 1],
                                  rowf[0:1, f * 128:(f + 1) * 128], ones11,
                                  start=(f == 0), stop=(f == nb - 1))
                mtk = token(mm, sT)
                rowf_free = mtk
                wait(v, mtk)
                etk = token(v.tensor_copy(dst_cols[:, r * 8:r * 8 + nb],
                                          psB[0][:, 0:nb]), sV)
                colld_prev = etk
            return etk

        bqc_tok = col_load(bq_c, dvec["bq"], NF)
        b1c_tok = col_load(b1_c, db1, NQ)
        wait(v, bqc_tok)
        nbq_tok = token(v.tensor_scalar(nbq_c[:], bq_c[:], -1.0, None, op0=ALU.mult), sV)

        def ln_stats(i, c, src_sl, dep_v, dep_s):
            sc = sc_sts[i][:, c:c + 1]
            sq = sq_sts[i][:, c:c + 1]
            wait(v, dep_v)
            t1 = token(v.reduce_sum(sc, src_sl, axis=AX.X), sV)
            wait(s, dep_s)
            t2 = token(s.activation(pool[c % 2][:], src_sl, AF.Square,
                                    accum_out=sq), sS)
            wait(v, t1)
            t3 = token(v.tensor_scalar(mu[i][:, c:c + 1], sc,
                                       1.0 / D, None, op0=ALU.mult), sV)
            wait(v, t2)
            t4 = token(v.tensor_scalar(sq, sq, 1.0 / D, None, op0=ALU.mult), sV)
            wait(v, t3)
            t5 = token(v.tensor_tensor(sc, mu[i][:, c:c + 1],
                                       mu[i][:, c:c + 1], op=ALU.mult), sV)
            wait(v, t4)
            wait(v, t5)
            t6 = token(v.tensor_tensor(sq, sq, sc, op=ALU.subtract), sV)
            wait(s, t6)
            wait(s, consts_tok)
            t7 = token(s.activation(sq, sq, AF.Ln, bias=epsln[:, 0:1]), sS)
            wait(s, t7)
            t8 = token(s.activation(rr[i][:, c:c + 1], sq, AF.Exp,
                                    scale=-0.5), sS)
            return t8, t2

        ST_LT = ExitStack()
        ln0_tm = ST_LT.enter_context(nc.sbuf_tensor("ln0_tm", [128, NT * D], BF16))
        ST_W = ExitStack()
        WA = ST_W.enter_context(nc.sbuf_tensor("WA", [128, NF * D], BF16))
        WB = ST_W.enter_context(nc.sbuf_tensor("WB", [128, NF * D], BF16))
        ST_BR = ExitStack()
        bk_row = ST_BR.enter_context(nc.sbuf_tensor("bk_row", [1, D], BF16))
        bv_row = ST_BR.enter_context(nc.sbuf_tensor("bv_row", [1, D], BF16))
        bo_row = ST_BR.enter_context(nc.sbuf_tensor("bo_row", [1, D], BF16))
        ST_A = ExitStack()
        ln0_fm = ST_A.enter_context(nc.sbuf_tensor("ln0_fm", [128, NF * L], BF16))

        def load_small_w(dst, name, brow_dst, brow_name, sem, gate=None):
            if gate is not None:
                wait(g, gate)
            for k in range(NF):
                tk = token(g.dma_start(out=dst[:, k * D:(k + 1) * D],
                                       in_=wr[name][k]), sem, 16)
            if brow_dst is not None:
                tk = token(g.dma_start(out=brow_dst[:1, :],
                                       in_=dvec[brow_name][None, :]), sem, 16)
            return tk

        wv_tok = load_small_w(WA, "wv", bv_row, "bv", sWA)
        wk_tok = load_small_w(WB, "wk", bk_row, "bk", sWB)

        ln0_fm_r = ln0_fm[:].rearrange("p (f n) -> p f n", n=L)
        x_consumed = [None] * NT
        ln0fm_tok = [None] * NT
        ps_tr = [None, None]
        STP_A = ExitStack()
        ps_tr[0] = STP_A.enter_context(nc.psum_tensor("psTA0", [128, 1024], BF16))
        ps_tr[1] = STP_A.enter_context(nc.psum_tensor("psTA1", [128, 1024], BF16))
        tr_evac = [None] * NT

        def emit_A1(c):
            sl = xslot[c % 2]
            if c >= 2:
                wait(sy, x_consumed[c - 2])
            dtk = token(sy.dma_start(out=sl[:], in_=xr[c]), sX[c % 2], 16)
            return ln_stats(0, c, sl[:], dtk, dtk)[0]

        def emit_A2(c, r_tok):
            sl = xslot[c % 2]
            wait(v, r_tok)
            t9 = token(v.tensor_scalar(ln0_tm[:, c * D:(c + 1) * D], sl[:],
                                       mu[0][:, c:c + 1], rr[0][:, c:c + 1],
                                       op0=ALU.subtract, op1=ALU.mult), sV)
            x_consumed[c] = t9
            bank = ps_tr[c % 2]
            wait(t, t9)
            if c >= 2:
                wait(t, tr_evac[c - 2])
            for f in range(NF):
                mm = t.matmul(bank[:, f * 128:(f + 1) * 128],
                              ln0_tm[:, c * D + f * 128: c * D + (f + 1) * 128],
                              eyeb[:], is_transpose=True,
                              start=(f == 0), stop=(f == NF - 1))
            t10 = token(mm, sT)
            if c % 2 == 0:
                wait(s, t10)
                t11 = token(s.copy(ln0_fm_r[:, :, c * 128:(c + 1) * 128],
                                   bank[:].rearrange("p (f n) -> p f n", n=128)),
                            sS)
            else:
                wait(v, t10)
                t11 = token(v.tensor_copy(
                    ln0_fm_r[:, :, c * 128:(c + 1) * 128],
                    bank[:].rearrange("p (f n) -> p f n", n=128)), sV)
            ln0fm_tok[c] = t11
            tr_evac[c] = t11

        ST_VK = ExitStack()
        V_tm = ST_VK.enter_context(nc.sbuf_tensor("V_tm", [128, NT * D], BF16))
        K_tm = ST_VK.enter_context(nc.sbuf_tensor("K_tm", [128, NT * D], BF16))
        bank_free = [colld_prev, None, None, None]

        def proj_chunk(c, dst_tm, Wsl, brow, w_tok, elu, dst_tok, prev_elu):
            wait(t, ln0fm_tok[c])
            for j2 in range(2):
                b = (c * 2 + j2) % 4
                bank = psB[b]
                wait(t, bank_free[b])
                wait(t, w_tok)
                for k in range(NF):
                    t.matmul(bank[:],
                             ln0_fm[:, k * L + c * 128: k * L + (c + 1) * 128],
                             Wsl[:, k * D + j2 * 512: k * D + (j2 + 1) * 512],
                             start=(k == 0), stop=False)
                mm = t.matmul(bank[:], onesrb[:1, :],
                              brow[:1, j2 * 512:(j2 + 1) * 512],
                              start=False, stop=True)
                mtk = token(mm, sT)
                dst = dst_tm[:, c * D + j2 * 512: c * D + (j2 + 1) * 512]
                if not elu:
                    wait(v, mtk)
                    etk = token(v.tensor_copy(dst, bank[:]), sV)
                    bank_free[b] = etk
                    dst_tok[c] = etk
                else:
                    p1 = pool[2][:, j2 * 512:(j2 + 1) * 512]
                    p2 = pool[3][:, j2 * 512:(j2 + 1) * 512]
                    wait(s, mtk)
                    wait(s, prev_elu[j2])
                    a1 = token(s.activation(p1, bank[:], AF.Relu, scale=-1.0), sS)
                    wait(s, a1)
                    a2 = token(s.activation(p1, p1, AF.Exp, scale=-1.0), sS)
                    wait(v, mtk)
                    wait(v, prev_elu[j2])
                    wait(v, a1)
                    a3 = token(v.tensor_scalar(p2, bank[:], 0.0, None,
                                               op0=ALU.max), sV)
                    bank_free[b] = [a1, a3]
                    wait(v, a2)
                    wait(v, a3)
                    a4 = token(v.tensor_tensor(dst, p1, p2, op=ALU.add), sV)
                    dst_tok[c] = a4
                    prev_elu[j2] = a4

        v_tok = [None] * NT
        k_tok = [None] * NT
        rA = {}
        rA[0] = emit_A1(0)
        if NT > 1:
            rA[1] = emit_A1(1)
        for c in range(NT):
            emit_A2(c, rA[c])
            if c + 2 < NT:
                rA[c + 2] = emit_A1(c + 2)
            proj_chunk(c, V_tm, WA, bv_row, wv_tok, False, v_tok, None)
        wa_free = now(sT)

        for c in range(NT):
            wait(t, tr_evac[c])
        STP_A.close()

        prev_elu_k = [None, None]
        for c in range(NT):
            proj_chunk(c, K_tm, WB, bk_row, wk_tok, True, k_tok, prev_elu_k)
        wb_free = now(sT)

        ST_KS = ExitStack()
        KS = ST_KS.enter_context(nc.sbuf_tensor("KS", [128, NPAIR * 160], BF16, side="right"))
        STP_C = ExitStack()
        psC = [STP_C.enter_context(nc.psum_tensor(f"psC{i}", [128, 512], F32))
               for i in range(2)]
        ks_tok = [None] * NPAIR
        ksz_tok = token(v.memset(KS[:], 0.0), sV)
        wait(t, v_tok[NT - 1])
        wait(t, k_tok[NT - 1])
        for p in range(NPAIR):
            bank = psC[p % 2]
            if p >= 2:
                wait(t, ks_tok[p - 2])
            last_mm = None
            for hh in range(2):
                h = 2 * p + hh
                po, tp = (0, None) if hh == 0 else (64, (0, 64))
                for c in range(NT):
                    kl = K_tm[:, c * D + h * DH: c * D + (h + 1) * DH]
                    vl = V_tm[:, c * D + h * DH: c * D + (h + 1) * DH]
                    t.matmul(bank[po:po + 64, 0:64], kl, vl,
                             start=(c == 0), stop=False, tile_position=tp)
                    last_mm = t.matmul(bank[po:po + 64, 64:65], kl, onescb[:],
                                       start=False, stop=(c == NT - 1),
                                       tile_position=tp)
            mtk = token(last_mm, sT)
            wait(v, mtk)
            wait(v, ksz_tok)
            token(v.tensor_copy(KS[0:64, p * 160:p * 160 + 65],
                                bank[0:64, 0:65]), sV)
            ks_tok[p] = token(v.tensor_copy(KS[64:128, p * 160 + 80:p * 160 + 145],
                                            bank[64:128, 0:65]), sV)
        for p in range(NPAIR):
            wait(t, ks_tok[p])
        STP_C.close()
        ST_VK.close()

        ST_Q = ExitStack()
        Q_fm = ST_Q.enter_context(nc.sbuf_tensor("Q_fm", [128, NF * L], BF16, side="right"))
        wq_tok = load_small_w(WA, "wq", None, None, sWA, gate=wa_free)
        wo_tok = load_small_w(WB, "wo", bo_row, "bo", sWB, gate=wb_free)
        q_tok = [None] * (NF * NTT)
        qi = 0
        prev_elu = [None, None]
        for fo in range(NF):
            for tt in range(NTT):
                b = qi % 4
                bank = psB[b]
                wait(t, bank_free[b])
                wait(t, wq_tok)
                wait(t, ln0fm_tok[NT - 1])
                for k in range(NF):
                    mm = t.matmul(bank[:],
                                  WA[:, k * D + fo * 128:k * D + (fo + 1) * 128],
                                  ln0_fm[:, k * L + tt * 512: k * L + (tt + 1) * 512],
                                  start=(k == 0), stop=(k == NF - 1))
                mtk = token(mm, sT)
                p1 = pool[2][:, (qi % 2) * 512:(qi % 2) * 512 + 512]
                p2 = pool[3][:, (qi % 2) * 512:(qi % 2) * 512 + 512]
                wait(s, mtk)
                wait(s, prev_elu[qi % 2])
                a1 = token(s.activation(p1, bank[:], AF.Relu, scale=-1.0,
                                        bias=nbq_c[:, fo:fo + 1]), sS)
                wait(s, a1)
                a2 = token(s.activation(p1, p1, AF.Exp, scale=-1.0), sS)
                wait(v, mtk)
                wait(v, nbq_tok)
                wait(v, prev_elu[qi % 2])
                wait(v, a1)
                a3 = token(v.tensor_scalar(p2, bank[:], bq_c[:, fo:fo + 1], 0.0,
                                           op0=ALU.add, op1=ALU.max), sV)
                bank_free[b] = [a1, a3]
                wait(v, a2)
                wait(v, a3)
                a4 = token(v.tensor_tensor(
                    Q_fm[:, fo * L + tt * 512: fo * L + (tt + 1) * 512],
                    p1, p2, op=ALU.add), sV)
                q_tok[qi] = a4
                prev_elu[qi % 2] = a4
                qi += 1
        q_done_T = None
        ST_A.close()
        ln0fm_last_reader = now(sT)

        ST_AF = ExitStack()
        A_fm = ST_AF.enter_context(nc.sbuf_tensor("A_fm", [128, NF * L], BF16))
        Abig = [ST_AF.enter_context(nc.sbuf_tensor("Abig0", [128, 1024], BF16))] * 2
        zz = ST_AF.enter_context(nc.sbuf_tensor("zz", [128, 4 * H], F32))
        STP_E = ExitStack()
        psE = [STP_E.enter_context(nc.psum_tensor(f"psE{i}", [128, 512], F32))
               for i in range(2)]
        psEt = [STP_E.enter_context(nc.psum_tensor(f"psEt{i}", [128, 1024], BF16))
                for i in range(2)]
        A_fm_r = A_fm[:].rearrange("p (f n) -> p f n", n=L)
        a_fm_tok = [None] * NT
        zfree = [None, None]
        sfree = [None, None]
        abig_free = [None, None]
        tb_free = [None, None]
        wait(t, q_tok[NF * NTT - 1])
        for c in range(NT):
            par = c % 2
            zbank = psB[2 * par]
            sb = [psE[par], psB[2 * par + 1]]
            wait(t, bank_free[2 * par])
            wait(t, zfree[par])
            zmm = None
            for h in range(H):
                p, hh = h // 2, h % 2
                zmm = t.matmul(zbank[:, h:h + 1],
                               Q_fm[:, p * L + c * 128:p * L + (c + 1) * 128],
                               KS[:, p * 160 + 80 * hh + 64: p * 160 + 80 * hh + 65],
                               start=(h == 0), stop=(h == H - 1))
            zmtk = token(zmm, sT)
            bank_free[2 * par] = None
            wait(t, bank_free[2 * par + 1])
            wait(t, sfree[par])
            smm = [None, None]
            for h in range(H):
                p, hh = h // 2, h % 2
                smm[h // 8] = t.matmul(
                    sb[h // 8][:, (h % 8) * 64:(h % 8) * 64 + 64],
                    Q_fm[:, p * L + c * 128:p * L + (c + 1) * 128],
                    KS[:, p * 160 + 80 * hh: p * 160 + 80 * hh + 64],
                    start=(h % 8 == 0), stop=(h % 8 == 7))
            smtk = [token(smm[0], sT), token(smm[1], sT)]
            bank_free[2 * par + 1] = None
            zc = zz[:, 32 * par:32 * par + 32]
            wait(v, zmtk)
            zv1 = token(v.tensor_scalar(zc[:, 0:16], zbank[:, 0:16],
                                        EPS_ATTN, None, op0=ALU.add), sV)
            zfree[par] = zv1
            wait(v, zv1)
            zv2 = token(v.reciprocal(zc[:, 16:32], zc[:, 0:16]), sV)
            ab = Abig[par]
            wait(v, zv2)
            wait(v, abig_free[par])
            wait(s, zv2)
            wait(s, abig_free[par])
            sc_toks = []
            for h in range(H):
                src = sb[h // 8][:, (h % 8) * 64:(h % 8) * 64 + 64]
                dstA = ab[:, h * 64:(h + 1) * 64]
                if h // 8 == 0:
                    wait(v, smtk[0])
                    sc_toks.append(token(v.tensor_scalar(
                        dstA, src, zc[:, 16 + h:17 + h], None, op0=ALU.mult), sV))
                else:
                    wait(s, smtk[1])
                    sc_toks.append(token(s.activation(
                        dstA, src, AF.Copy, scale=zc[:, 16 + h:17 + h]), sS))
            sfree[par] = [sc_toks[7], sc_toks[15]]
            bank_t = psEt[par]
            wait(t, sc_toks[7])
            wait(t, sc_toks[15])
            wait(t, tb_free[par])
            tmm = None
            for p in range(NPAIR):
                tmm = t.matmul(bank_t[:, p * 128:(p + 1) * 128],
                               ab[:, p * 128:(p + 1) * 128],
                               eyeb[:], is_transpose=True,
                               start=(p == 0), stop=(p == NPAIR - 1))
            abig_free[par] = token(tmm, sT)
            wait(s, abig_free[par])
            etk = token(s.copy(A_fm_r[:, :, c * 128:(c + 1) * 128],
                               bank_t[:].rearrange("p (f n) -> p f n", n=128)), sS)
            tb_free[par] = etk
            a_fm_tok[c] = etk
        wait(t, tb_free[0])
        wait(t, tb_free[1])
        bank_free = [zfree[0], sfree[0][1], zfree[1], sfree[1][1]]
        STP_E.close()
        ST_Q.close()
        ST_KS.close()

        ST_U2 = ExitStack()
        u2_tm = ST_U2.enter_context(nc.sbuf_tensor("u2_tm", [128, NT * D], BF16, side="right"))
        ST_U1 = ExitStack()
        u1 = ST_U1.enter_context(nc.sbuf_tensor("u1", [128, NT * D], BF16, side="right"))
        ST_BC = ExitStack()
        g1B = ST_BC.enter_context(nc.sbuf_tensor("g1B", [128, D], BF16, side="right"))
        be1B = ST_BC.enter_context(nc.sbuf_tensor("be1B", [128, D], BF16, side="right"))
        u1_tok = [None] * NT
        row_prev = None

        def bcast_row_pre(dst, name):
            nonlocal row_prev
            wait(sy, row_prev)
            wait(sy, rowf_free)
            rtk = token(sy.dma_start(out=rowf[:1, :], in_=dvec[name][None, :]), sROW, 16)
            etk = None
            for j2 in range(2):
                bank = psB[j2]
                wait(t, rtk)
                wait(t, bank_free[j2])
                mm = t.matmul(bank[:], onesrf[:1, :],
                              rowf[:1, j2 * 512:(j2 + 1) * 512],
                              start=True, stop=True)
                mtk = token(mm, sT)
                wait(v, mtk)
                etk = token(v.tensor_copy(dst[:, j2 * 512:(j2 + 1) * 512],
                                          bank[:]), sV)
                bank_free[j2] = etk
            row_prev = etk
            return etk

        bcast_row_pre(g1B, "g1")
        g1b_tok = bcast_row_pre(be1B, "be1")
        token(g.dma_start(out=b2_row[:1, :], in_=dvec["b2"][None, :]), sMISC, 16)
        b2row_tok = now(sMISC)

        u2_tok = [None] * NT
        prev_ap = [None, None]
        for c in range(NT):
            wait(t, a_fm_tok[c])
            for j2 in range(2):
                b = (c * 2 + j2) % 4
                bank = psB[b]
                wait(t, bank_free[b])
                wait(t, wo_tok)
                for k in range(NF):
                    t.matmul(bank[:],
                             A_fm[:, k * L + c * 128: k * L + (c + 1) * 128],
                             WB[:, k * D + j2 * 512: k * D + (j2 + 1) * 512],
                             start=(k == 0), stop=False)
                mm = t.matmul(bank[:], onesrb[:1, :],
                              bo_row[:1, j2 * 512:(j2 + 1) * 512],
                              start=False, stop=True)
                mtk = token(mm, sT)
                wait(v, mtk)
                etk = token(v.tensor_tensor(
                    u1[:, c * D + j2 * 512: c * D + (j2 + 1) * 512],
                    bank[:],
                    ln0_tm[:, c * D + j2 * 512: c * D + (j2 + 1) * 512],
                    op=ALU.add), sV)
                bank_free[b] = etk
                u1_tok[c] = etk
            cd = slice(c * D, (c + 1) * D)
            r_tok, _ = ln_stats(1, c, u1[:, cd], u1_tok[c], u1_tok[c])
            po = pool[2 + (c % 2)]
            wait(v, r_tok)
            wait(v, prev_ap[c % 2])
            wait(v, g1b_tok)
            t9 = token(v.scalar_tensor_tensor(
                po[:], u1[:, cd], mu[1][:, c:c + 1], g1B[:],
                op0=ALU.subtract, op1=ALU.mult), sV)
            wait(v, t9)
            t11 = token(v.scalar_tensor_tensor(
                u2_tm[:, cd], po[:], rr[1][:, c:c + 1], be1B[:],
                op0=ALU.mult, op1=ALU.add), sV)
            u2_tok[c] = t11
            prev_ap[c % 2] = t11
        wo_last = now(sT)
        ST_AF.close()
        ST_BR.close()
        ST_W.close()
        ST_LT.close()

        ST_BC.close()
        ST_U1.close()

        g2B = ST_U2.enter_context(nc.sbuf_tensor("g2B", [128, D], BF16, side="right"))
        be2B = ST_U2.enter_context(nc.sbuf_tensor("be2B", [128, D], BF16, side="right"))
        fbB = ST_U2.enter_context(nc.sbuf_tensor("fbB", [128, D], F32, side="right"))
        bcast_row_pre(g2B, "g2")
        bcast_row_pre(be2B, "be2")
        bc_tok = bcast_row_pre(fbB, "fb")

        ST_WFQ = ExitStack()
        wfH = [ST_WFQ.enter_context(nc.sbuf_tensor(f"wfH{i}", [128, D], F32))
               for i in range(2)]
        wfa_h = [None, None]
        wf_comb = [None]

        def emit_wfamax(ks):
            for k in ks:
                ws = wfH[k % 2]
                wait(sy, wfa_h[k % 2])
                dtk = token(sy.dma_start(out=ws[:], in_=wr["wf"][k]), sW1[k % 2], 16)
                wait(v, dtk)
                wait(v, wf_comb[0])
                col = smalls[:, 4:5] if k == 0 else smalls[:, 5:6]
                ctk = token(v.reduce_max(col, ws[:], axis=AX.X,
                                         apply_absolute_value=True), sV)
                if k > 0:
                    wait(v, ctk)
                    ctk = token(v.tensor_tensor(smalls[:, 4:5], smalls[:, 4:5],
                                                smalls[:, 5:6], op=ALU.max), sV)
                wf_comb[0] = ctk
                wfa_h[k % 2] = ctk

        ST_W1 = ExitStack()
        W1res = ST_W1.enter_context(nc.sbuf_tensor("W1res", [128, NF * DFF], BF16))
        wait(g, wo_last)
        wait(g, u1_tok[NT - 1])
        w1_tok = None
        for k in range(NF):
            w1_tok = token(g.dma_start(out=W1res[:, k * DFF:(k + 1) * DFF],
                                       in_=w1r[k]), sWA, 16)

        barrier()

        ST_H = ExitStack()
        NW2 = 5
        W2buf = ST_H.enter_context(nc.sbuf_tensor("W2buf", [128, NW2 * D], BF16))
        u2fm = ST_H.enter_context(nc.sbuf_tensor("u2fm", [128, NF * 512], BF16))
        y1 = ST_H.enter_context(nc.sbuf_tensor("y1", [128, NQ * 512], BF16))
        h1t = [ST_H.enter_context(nc.sbuf_tensor(f"h1t{i}", [128, D], F32))
               for i in range(2)]
        u2fm_r = u2fm[:].rearrange("p (f n) -> p f n", n=512)

        h1_tok = [None] * NT
        am_tok = [None] * NT
        x2_consumed = [None] * NT
        w2_mm_tok = [None] * NQ
        psH = None
        w1buf_gate = [None, None]
        STP_T = None
        psum_evac_prev = None
        w2_mm_prev = None
        fm_done = [None]

        def emit_u2fm(tt2):
            STP_T2 = ExitStack()
            psT = STP_T2.enter_context(nc.psum_tensor(f"psTH{tt2}", [128, 1024], BF16))
            fm_last = None
            for ci in range(4):
                c = tt2 * 4 + ci
                wait(t, u2_tok[c])
                wait(t, fm_last)
                wait(t, psum_evac_prev)
                for f in range(NF):
                    mm = t.matmul(psT[:, f * 128:(f + 1) * 128],
                                  u2_tm[:, c * D + f * 128: c * D + (f + 1) * 128],
                                  eyeb[:], is_transpose=True,
                                  start=(f == 0), stop=(f == NF - 1))
                mtk = token(mm, sT)
                wait(s, mtk)
                fm_last = token(s.copy(
                    u2fm_r[:, :, ci * 128:(ci + 1) * 128],
                    psT[:].rearrange("p (f n) -> p f n", n=128)), sS)
            wait(t, fm_last)
            STP_T2.close()
            fm_done[0] = fm_last

        def emit_W1(tt):
            y1_tok = [None] * NQ
            wait(t, w1_tok)
            for i in range(NQ):
                b = i % 4
                bank = psB[b]
                wait(t, bank_free[b])
                for k in range(NF):
                    mm = t.matmul(
                        bank[:],
                        W1res[:, k * DFF + i * 128: k * DFF + (i + 1) * 128],
                        u2fm[:, k * 512:(k + 1) * 512],
                        start=(k == 0), stop=(k == NF - 1))
                mtk = token(mm, sT)
                if i % 2 == 0:
                    wait(v, mtk)
                    etk = token(v.tensor_scalar(
                        y1[:, i * 512:(i + 1) * 512], bank[:],
                        b1_c[:, i:i + 1], 0.0, op0=ALU.add, op1=ALU.max), sV)
                else:
                    wait(s, mtk)
                    etk = token(s.activation(
                        y1[:, i * 512:(i + 1) * 512], bank[:],
                        AF.Relu, bias=b1_c[:, i:i + 1]), sS)
                bank_free[b] = etk
                y1_tok[i] = etk
            return y1_tok

        def emit_W2(tt, y1_tok):
            nonlocal psum_evac_prev, w2_mm_prev
            STP_H = ExitStack()
            psH = [STP_H.enter_context(nc.psum_tensor(f"psH{tt}_{i}", [128, 512], F32))
                   for i in range(4)]
            psW2 = psB + psH
            for b in range(4):
                wait(t, bank_free[b])
                bank_free[b] = None
            last_mm = {}
            for k2 in range(NQ):
                sl = k2 % NW2
                if k2 >= NW2:
                    wait(g, w2_mm_tok[k2 - NW2])
                elif w2_mm_prev is not None:
                    wait(g, w2_mm_prev)
                dtk = token(g.dma_start(out=W2buf[:, sl * D:(sl + 1) * D],
                                        in_=w2r[k2]), sW2[sl], 16)
                wait(t, dtk)
                wait(t, y1_tok[k2])
                for c2 in range(4):
                    for j2 in range(2):
                        bank = psW2[c2 * 2 + j2]
                        last_mm[(c2, j2)] = t.matmul(
                            bank[:],
                            y1[:, k2 * 512 + c2 * 128: k2 * 512 + (c2 + 1) * 128],
                            W2buf[:, sl * D + j2 * 512: sl * D + (j2 + 1) * 512],
                            start=(k2 == 0), stop=False)
                w2_mm_tok[k2] = token(last_mm[(3, 1)], sT)
            ev_toks = {}
            wait(t, b2row_tok)
            for c2 in range(4):
                for j2 in range(2):
                    bank = psW2[c2 * 2 + j2]
                    mm = t.matmul(bank[:], onesrb[:1, :],
                                  b2_row[:1, j2 * 512:(j2 + 1) * 512],
                                  start=False, stop=True)
                    mtk = token(mm, sT)
                    c = tt * 4 + c2
                    wait(v, mtk)
                    etk = token(v.tensor_tensor(
                        u2_tm[:, c * D + j2 * 512: c * D + (j2 + 1) * 512],
                        bank[:],
                        u2_tm[:, c * D + j2 * 512: c * D + (j2 + 1) * 512],
                        op=ALU.add), sV)
                    ev_toks[(c2, j2)] = etk
            psum_evac_prev = ev_toks[(3, 1)]
            w2_mm_prev = w2_mm_tok[NQ - 1]
            for b in range(4):
                bank_free[b] = ev_toks[(b // 2, b % 2)]
            STP_H.close()
            return ev_toks

        def emit_LN2(tt, ev_toks):
            prev_ap2 = [None]
            prev_am = [None]
            st1 = {}

            def ln2_part1(c2):
                c = tt * 4 + c2
                cd = slice(c * D, (c + 1) * D)
                dep = ev_toks[(c2, 1)]
                wait(v, ev_toks[(c2, 0)])
                wait(s, ev_toks[(c2, 0)])
                r_tok, _ = ln_stats(2, c, u2_tm[:, cd], dep, dep)
                wait(v, r_tok)
                wait(v, prev_ap2[0])
                t9 = token(v.scalar_tensor_tensor(
                    pool[2][:], u2_tm[:, cd], mu[2][:, c:c + 1], g2B[:],
                    op0=ALU.subtract, op1=ALU.mult), sV)
                sl = xslot[c % 2]
                if c >= 2:
                    wait(sy, x2_consumed[c - 2])
                else:
                    wait(sy, x_consumed[NT - 1])
                dtk = token(sy.dma_start(out=sl[:], in_=xr[c]), sX[c % 2], 16)
                if c >= 2:
                    wait(v, x2_consumed[c - 2])
                wait(v, t9)
                t11 = token(v.scalar_tensor_tensor(
                    u2_tm[:, cd], pool[2][:], rr[2][:, c:c + 1], be2B[:],
                    op0=ALU.mult, op1=ALU.add), sV)
                prev_ap2[0] = t11
                wait(v, dtk)
                wait(v, t11)
                t12 = token(v.tensor_tensor(h1t[c % 2][:], u2_tm[:, cd], sl[:],
                                            op=ALU.add), sV)
                h1_tok[c] = t12
                r3_tok, _ = ln_stats(3, c, h1t[c % 2][:], t12, t12)
                st1[c2] = (c, t12, r3_tok)

            def ln2_part2(c2):
                c, t12, r3_tok = st1[c2]
                wait(s, r3_tok)
                wait(s, prev_am[0])
                t21 = token(s.activation(pool[3][:], h1t[c % 2][:], AF.Identity,
                                         bias=mu[3][:, c:c + 1], scale=-1.0), sS)
                wait(v, t21)
                t22 = token(v.reduce_max(smalls[:, 2:3], pool[3][:], axis=AX.X,
                                         apply_absolute_value=True), sV)
                wait(v, t22)
                t23 = token(v.tensor_tensor(am_c[:, c:c + 1], smalls[:, 2:3],
                                            rr[3][:, c:c + 1], op=ALU.mult), sV)
                am_tok[c] = t23
                prev_am[0] = [t23, t21]
                x2_consumed[c] = [t12, t23, t21, r3_tok]

            for c2 in range(4):
                ln2_part1(c2)
                if c2 >= 1:
                    ln2_part2(c2 - 1)
            ln2_part2(3)

        emit_u2fm(0)
        y1_cur = emit_W1(0)
        for tt in range(NTT):
            ev_cur = emit_W2(tt, y1_cur)
            if tt + 1 < NTT:
                emit_u2fm(tt + 1)
                y1_cur = emit_W1(tt + 1)
            emit_LN2(tt, ev_cur)
            wfp = -(-NF // max(1, min(NTT, 3)))
            emit_wfamax(list(range(NF))[tt * wfp:(tt + 1) * wfp])

        wq_dtk = [None] * NF
        for k in range(2):
            wait(sy, wfa_h[k % 2])
            wq_dtk[k] = token(sy.dma_start(out=wfH[k][:], in_=wr["wf"][k]),
                              sW1[k % 2], 16)

        wait(sy, now(sT))
        wait(sy, now(sV))
        wait(sy, now(sS))
        ST_H.close()
        ST_W1.close()

        ST_I = ExitStack()
        wfq = ST_I.enter_context(nc.sbuf_tensor("wfq", [128, NF * D], BF16))
        wait(v, am_tok[NT - 1])
        t1 = token(v.reduce_max(smalls[:, 3:4], am_c[:, 0:NT], axis=AX.X), sV)
        wait(t, t1)
        mm = t.matmul(psB[0][:1, 0:128], smalls[:, 3:4], eyef[:], start=True, stop=True)
        t2 = token(mm, sT)
        wait(v, t2)
        t3 = token(v.tensor_copy(row128[:1, :], psB[0][:1, 0:128]), sV)
        wait(v, t3)
        t4 = token(v.reduce_max(sc1[:1, 0:1], row128[:1, :], axis=AX.X), sV)
        t5 = token(v.memset(pad[:1, :], 0.0), sV)
        wait(v, t4)
        wait(v, t5)
        t6 = token(v.tensor_copy(pad[:1, 0:1], sc1[:1, 0:1]), sV)
        wait(g, t6)
        t7 = token(g.dma_start(out=cc_in[:, :], in_=pad[:1, :]), sCC, 16)
        wait(g, t7)
        cctk = token(g.collective_compute(
            "AllReduce", ALU.max,
            replica_groups=[list(range(N_CORES))],
            ins=[cc_in[:, :]], outs=[cc_out[:, :]]), sCCP)

        wait(t, wf_comb[0])
        mm = t.matmul(psB[1][:1, 0:128], smalls[:, 4:5], eyef[:], start=True, stop=True)
        t9 = token(mm, sT)
        wait(v, t9)
        t10 = token(v.tensor_copy(row128[:1, :], psB[1][:1, 0:128]), sV)
        wait(v, t10)
        t11 = token(v.reduce_max(sc1[:1, 1:2], row128[:1, :], axis=AX.X), sV)
        wait(v, t11)
        t13 = token(v.tensor_scalar(sc1[:1, 1:2], sc1[:1, 1:2], 1e-12, None,
                                    op0=ALU.max), sV)
        wait(v, t13)
        t14a = token(v.reciprocal(sc1[:1, 2:3], sc1[:1, 1:2]), sV)
        wait(v, t14a)
        t14 = token(v.tensor_scalar(sc1[:1, 2:3], sc1[:1, 2:3], 448.0, None,
                                    op0=ALU.mult), sV)
        t15 = token(v.tensor_scalar(sc1[:1, 3:4], sc1[:1, 1:2], 1.0 / 448.0, None,
                                    op0=ALU.mult), sV)
        wait(t, t14)
        mm = t.matmul(psB[2][:, 0:1], onesrf[:1, :], sc1[:1, 2:3], start=True, stop=True)
        t16 = token(mm, sT)
        wait(v, t16)
        t17 = token(v.tensor_copy(smalls[:, 6:7], psB[2][:, 0:1]), sV)
        wsI = [ST_I.enter_context(nc.sbuf_tensor(f"wsI{i}", [128, D], F32))
               for i in range(2)]
        wslots = [wfH[0], wfH[1], wsI[0], wsI[1]]
        wsem = [sW1[0], sW1[1], sWQ[0], sWQ[1]]
        jwfq_tok = [None] * NF
        wfa = [None] * 4
        qv_prev = None
        qg_prev = None
        for k in range(NF):
            ws = wslots[k % 4]
            if wq_dtk[k] is None:
                wait(sy, wfa[k % 4])
                wq_dtk[k] = token(sy.dma_start(out=ws[:], in_=wr["wf"][k]),
                                  wsem[k % 4], 16)
            wait(v, wq_dtk[k])
            wait(v, t17)
            wait(v, qg_prev)
            q1 = token(v.tensor_scalar(pool[0][:], ws[:], smalls[:, 6:7], None,
                                       op0=ALU.mult), sV)
            wfa[k % 4] = q1
            wait(s, q1)
            wait(s, qv_prev)
            q2 = token(s.mul(pool[1][:], pool[0][:], CV), sS)
            wait(g, q2)
            wait(g, qv_prev)
            q3 = token(g.tensor_tensor(pool[2][:], pool[1][:], pool[0][:],
                                       op=ALU.subtract), sP)
            qg_prev = q3
            wait(v, q3)
            q4 = token(v.tensor_tensor(wfq[:, k * D:(k + 1) * D], pool[1][:],
                                       pool[2][:], op=ALU.subtract), sV)
            qv_prev = q4
            jwfq_tok[k] = q4
        wfq_tok = now(sV)

        wait(g, cctk)
        ccb = token(g.dma_start(out=pad[:1, :], in_=cc_out[:, :]), sCC, 16)
        wait(v, ccb)
        t19 = token(v.tensor_scalar(sc1[:1, 4:5], pad[:1, 0:1], 1e-12, None,
                                    op0=ALU.max), sV)
        wait(v, t19)
        t20a = token(v.reciprocal(sc1[:1, 5:6], sc1[:1, 4:5]), sV)
        wait(v, t20a)
        t20 = token(v.tensor_scalar(sc1[:1, 5:6], sc1[:1, 5:6], 448.0, None,
                                    op0=ALU.mult), sV)
        t21 = token(v.tensor_scalar(sc1[:1, 6:7], sc1[:1, 4:5], 1.0 / 448.0, None,
                                    op0=ALU.mult), sV)
        wait(v, t21)
        t22 = token(v.tensor_tensor(sc1[:1, 7:8], sc1[:1, 6:7], sc1[:1, 3:4],
                                    op=ALU.mult), sV)
        wait(t, t20)
        mm = t.matmul(psB[3][:, 0:1], onesrf[:1, :], sc1[:1, 5:6], start=True, stop=True)
        token(mm, sT)
        wait(t, t22)
        mm2 = t.matmul(psB[3][:, 1:2], onesrf[:1, :], sc1[:1, 7:8], start=True, stop=True)
        t23 = token(mm2, sT)
        wait(v, t23)
        t24 = token(v.tensor_copy(sxB[:], psB[3][:, 0:2]), sV)
        wait(v, t24)
        rs_tok = token(v.tensor_scalar(rs3[:, 0:NT], rr[3][:, 0:NT],
                                       sxB[:, 0:1], None, op0=ALU.mult), sV)

        xq_fm = ST_I.enter_context(nc.sbuf_tensor("xq_fm", [128, NF * L], BF16))
        qtm = [ST_I.enter_context(nc.sbuf_tensor(f"qtm{i}", [128, D], BF16))
               for i in range(4)]
        outst = [ST_I.enter_context(nc.sbuf_tensor(f"outst{i}", [128, D], F32))
                 for i in range(2)]
        h1j = [ST_I.enter_context(nc.sbuf_tensor(f"h1j{i}", [128, D], F32))
               for i in range(4)]
        STP_J = ExitStack()
        psJA = [STP_J.enter_context(nc.psum_tensor(f"psJA{i}", [128, 1024], BF16))
                for i in range(2)]
        psJB = [STP_J.enter_context(nc.psum_tensor(f"psJB{i}", [128, 1024], BF16))
                for i in range(2)]
        xq_fm_r = xq_fm[:].rearrange("p (f n) -> p f n", n=L)
        xqfm_tok = [None] * NT
        qtm_free = [None] * 4
        jq_prev = [None, None]
        jtr_prev = [None, None]
        store_tok = [None] * NT
        ost_free = [None, None]
        hj_free = [None] * 4
        xsl_free = [None, None]
        hb_tok = [None] * NT
        jq4 = [None] * NT

        def emit_chain(c):
            cd = slice(c * D, (c + 1) * D)
            if c >= 2:
                wait(sy, xsl_free[c % 2])
            else:
                wait(sy, x2_consumed[NT - 2 + c])
            dtk = token(sy.dma_start(out=xslot[c % 2][:], in_=xr[c]), sX[c % 2], 16)
            wait(g, dtk)
            wait(g, hj_free[c % 4] if c >= 4 else None)
            hj = token(g.tensor_tensor(h1j[c % 4][:], u2_tm[:, cd],
                                       xslot[c % 2][:], op=ALU.add), sP)
            xsl_free[c % 2] = hj
            wait(v, jq_prev[c % 2])
            wait(v, rs_tok)
            wait(v, hj)
            pa = pool[2 * (c % 2)]
            pb = pool[2 * (c % 2) + 1]
            q1 = token(v.tensor_scalar(pa[:], h1j[c % 4][:],
                                       mu[3][:, c:c + 1], rs3[:, c:c + 1],
                                       op0=ALU.subtract, op1=ALU.mult), sV)
            wait(s, q1)
            q2 = token(s.mul(pb[:], pa[:], CV), sS)
            wait(g, q2)
            q3 = token(g.tensor_tensor(pa[:], pb[:], pa[:],
                                       op=ALU.subtract), sP)
            wait(v, qtm_free[c % 4])
            wait(v, q3)
            q4 = token(v.tensor_tensor(qtm[c % 4][:], pb[:], pa[:],
                                       op=ALU.subtract), sV)
            jq_prev[c % 2] = q4
            jq4[c] = q4
            wait(g, bc_tok)
            wait(g, q1)
            hb_tok[c] = token(g.tensor_tensor(h1j[c % 4][:], h1j[c % 4][:],
                                              fbB[:], op=ALU.add), sP)

        def emit_tr(c):
            wait(t, jq4[c])
            wait(t, jtr_prev[c % 2])
            pa_ = psJA[c % 2]
            pb_ = psJB[c % 2]
            for f in range(4):
                mm = t.matmul(pa_[:, f * 128:(f + 1) * 128],
                              qtm[c % 4][:, f * 128:(f + 1) * 128],
                              eyeb[:], is_transpose=True,
                              start=(f == 0), stop=(f == 3))
            tA = token(mm, sT)
            for f in range(4):
                mm = t.matmul(pb_[:, f * 128:(f + 1) * 128],
                              qtm[c % 4][:, (4 + f) * 128:(5 + f) * 128],
                              eyeb[:], is_transpose=True,
                              start=(f == 0), stop=(f == 3))
            tB = token(mm, sT)
            wait(s, tA)
            e1 = token(s.copy(xq_fm_r[:, 0:4, c * 128:(c + 1) * 128],
                              pa_[:, 0:512].rearrange("p (f n) -> p f n", n=128)),
                       sS)
            wait(v, tB)
            e2 = token(v.tensor_copy(
                xq_fm_r[:, 4:8, c * 128:(c + 1) * 128],
                pb_[:, 0:512].rearrange("p (f n) -> p f n", n=128)), sV)
            t11 = [e1, e2]
            xqfm_tok[c] = t11
            qtm_free[c % 4] = t11
            jtr_prev[c % 2] = t11

        def emit_jmm(c):
            t11 = xqfm_tok[c]
            ost = outst[c % 2]
            ev2 = None
            for j2 in range(2):
                b = (c * 2 + j2) % 4
                bank = psB[b]
                wait(t, bank_free[b])
                wait(t, t11)
                for k in range(NF):
                    wait(t, jwfq_tok[k])
                    mm = t.matmul(bank[:],
                                  xq_fm[:, k * L + c * 128: k * L + (c + 1) * 128],
                                  wfq[:, k * D + j2 * 512:k * D + (j2 + 1) * 512],
                                  start=(k == 0), stop=(k == NF - 1))
                mtk = token(mm, sT)
                wait(v, mtk)
                wait(v, hb_tok[c])
                wait(v, ost_free[c % 2])
                ev2 = token(v.scalar_tensor_tensor(
                    ost[:, j2 * 512:(j2 + 1) * 512], bank[:],
                    sxB[:, 1:2], h1j[c % 4][:, j2 * 512:(j2 + 1) * 512],
                    op0=ALU.mult, op1=ALU.add), sV)
                bank_free[b] = ev2
            hj_free[c % 4] = ev2
            wait(g, ev2)
            store_tok[c] = token(g.dma_start(out=yr[c], in_=ost[:]), sOUT[c % 2], 16)
            ost_free[c % 2] = store_tok[c]

        emit_chain(0)
        emit_chain(1)
        emit_chain(2)
        emit_tr(0)
        for c in range(NT):
            if c + 3 < NT:
                emit_chain(c + 3)
            if c + 1 < NT:
                emit_tr(c + 1)
            emit_jmm(c)

        wait(g, now(sOUT[0]))
        wait(g, now(sOUT[1]))
        STP_J.close()
        ST_I.close()
        ST_WFQ.close()
        ST_U2.close()

    return nc


_CACHE = {}


def _consts():
    import ml_dtypes
    eye_f = np.eye(128, dtype=np.float32)
    return {
        "eyeb": eye_f.astype(ml_dtypes.bfloat16),
        "eyef": eye_f,
        "onescb": np.ones((128, 1), dtype=ml_dtypes.bfloat16),
        "onesrf": np.ones((1, 128), dtype=np.float32),
        "onesrb": np.ones((1, 128), dtype=ml_dtypes.bfloat16),
    }


def make_in_maps(inputs):
    x = np.ascontiguousarray(np.asarray(inputs["L0_attn_x"], dtype=np.float32))
    N = x.shape[0]
    base = {
        "wq": np.ascontiguousarray(inputs["L0_attn_Wq"], dtype=np.float32),
        "wk": np.ascontiguousarray(inputs["L0_attn_Wk"], dtype=np.float32),
        "wv": np.ascontiguousarray(inputs["L0_attn_Wv"], dtype=np.float32),
        "wo": np.ascontiguousarray(inputs["L0_attn_Wo"], dtype=np.float32),
        "w1": np.ascontiguousarray(inputs["L0_attn_W1"], dtype=np.float32),
        "w2": np.ascontiguousarray(inputs["L0_attn_W2"], dtype=np.float32),
        "wf": np.ascontiguousarray(np.asarray(inputs["L0_ffn_weight"], dtype=np.float32).T),
        "bq": np.ascontiguousarray(inputs["L0_attn_bq"], dtype=np.float32),
        "bk": np.ascontiguousarray(inputs["L0_attn_bk"], dtype=np.float32),
        "bv": np.ascontiguousarray(inputs["L0_attn_bv"], dtype=np.float32),
        "bo": np.ascontiguousarray(inputs["L0_attn_bo"], dtype=np.float32),
        "b1": np.ascontiguousarray(inputs["L0_attn_b1"], dtype=np.float32),
        "b2": np.ascontiguousarray(inputs["L0_attn_b2"], dtype=np.float32),
        "fb": np.ascontiguousarray(inputs["L0_ffn_bias"], dtype=np.float32),
        "g1": np.ascontiguousarray(inputs["L0_attn_g1"], dtype=np.float32),
        "be1": np.ascontiguousarray(inputs["L0_attn_beta1"], dtype=np.float32),
        "g2": np.ascontiguousarray(inputs["L0_attn_g2"], dtype=np.float32),
        "be2": np.ascontiguousarray(inputs["L0_attn_beta2"], dtype=np.float32),
        "epsln": np.full((128, 1), EPS_LN, dtype=np.float32),
        **_consts(),
    }
    return [dict(base, x=np.ascontiguousarray(x[n])) for n in range(N)]


def kernel(**inputs):
    x = np.asarray(inputs["L0_attn_x"])
    N, L, Dm = x.shape
    assert Dm == D and N == N_CORES
    if L not in _CACHE:
        _CACHE[L] = build(L)
    nc = _CACHE[L]
    in_maps = make_in_maps(inputs)
    res = run_bass_kernel_spmd(nc, in_maps, core_ids=list(range(N_CORES)))
    return np.stack([np.asarray(r["y"]) for r in res.results], axis=0)


# version 30 (final)
# speedup vs baseline: 1.0232x; 1.0232x over previous
import numpy as np
from contextlib import ExitStack

import concourse.bass as bass
import concourse.mybir as mybir
from concourse.bass_utils import run_bass_kernel_spmd

F32 = mybir.dt.float32
BF16 = mybir.dt.bfloat16
AF = mybir.ActivationFunctionType
ALU = mybir.AluOpType
AX = mybir.AxisListType

D = 1024
NF = 8
H = 16
DH = 64
NPAIR = 8
DFF = 4096
NQ = 32
EPS_LN = 1e-5
EPS_ATTN = 1e-6
CV = float(2 ** 20 + 1)
N_CORES = 8


def build(L):
    NT = L // 128
    NTT = L // 512
    nc = bass.Bass("TRN2", target_bir_lowering=False)

    dx = nc.dram_tensor("x", [L, D], F32, kind="ExternalInput").ap()
    dws = {n: nc.dram_tensor(n, [D, D], F32, kind="ExternalInput").ap()
           for n in ("wq", "wk", "wv", "wo", "wf")}
    dw1 = nc.dram_tensor("w1", [D, DFF], F32, kind="ExternalInput").ap()
    dw2 = nc.dram_tensor("w2", [DFF, D], F32, kind="ExternalInput").ap()
    dvec = {n: nc.dram_tensor(n, [D], F32, kind="ExternalInput").ap()
            for n in ("bq", "bk", "bv", "bo", "b2", "fb", "g1", "be1", "g2", "be2")}
    db1 = nc.dram_tensor("b1", [DFF], F32, kind="ExternalInput").ap()
    deyeb = nc.dram_tensor("eyeb", [128, 128], BF16, kind="ExternalInput").ap()
    deyef = nc.dram_tensor("eyef", [128, 128], F32, kind="ExternalInput").ap()
    donescb = nc.dram_tensor("onescb", [128, 1], BF16, kind="ExternalInput").ap()
    donesrf = nc.dram_tensor("onesrf", [1, 128], F32, kind="ExternalInput").ap()
    donesrb = nc.dram_tensor("onesrb", [1, 128], BF16, kind="ExternalInput").ap()
    depsln = nc.dram_tensor("epsln", [128, 1], F32, kind="ExternalInput").ap()
    dy = nc.dram_tensor("y", [L, D], F32, kind="ExternalOutput").ap()
    cc_in = nc.dram_tensor("cc_in", [1, 512], F32, kind="Internal").ap()
    cc_out = nc.dram_tensor("cc_out", [1, 512], F32, kind="Internal").ap()

    xr = dx.rearrange("(c p) n -> c p n", p=128)
    yr = dy.rearrange("(c p) n -> c p n", p=128)
    wr = {n: a.rearrange("(k p) n -> k p n", p=128) for n, a in dws.items()}
    w1r = dw1.rearrange("(k p) n -> k p n", p=128)
    w2r = dw2.rearrange("(k p) n -> k p n", p=128)

    t = nc.tensor
    v = nc.vector
    s = nc.scalar
    g = nc.gpsimd
    sy = nc.sync

    cnt = {}

    def token(inst, sem, k=1):
        inst.then_inc(sem, k)
        cnt[sem.name] = cnt.get(sem.name, 0) + k
        return (sem, cnt[sem.name])

    last_wait = {}

    def wait(eng, tok):
        if tok is None:
            return
        if isinstance(tok, list):
            for tk in tok:
                wait(eng, tk)
            return
        sem, val = tok
        key = (id(eng), sem.name)
        if last_wait.get(key, -1) >= val:
            return
        eng.wait_ge(sem, val)
        last_wait[key] = val

    def now(sem):
        return (sem, cnt.get(sem.name, 0))

    ST = ExitStack()
    with ST:
        E = ST.enter_context
        sCON = E(nc.semaphore("sCON"))
        sWA = E(nc.semaphore("sWA"))
        sWB = E(nc.semaphore("sWB"))
        sX = [E(nc.semaphore(f"sX{i}")) for i in range(2)]
        sW1 = [E(nc.semaphore(f"sW1_{i}")) for i in range(2)]
        sW2 = [E(nc.semaphore(f"sW2_{i}")) for i in range(5)]
        sROW = E(nc.semaphore("sROW"))
        sOUT = [E(nc.semaphore(f"sOUT{i}")) for i in range(2)]
        sCC = E(nc.semaphore("sCC"))
        sMISC = E(nc.semaphore("sMISC"))
        sWQ = [E(nc.semaphore(f"sWQ{i}")) for i in range(2)]
        sCCP = E(nc.semaphore("sCCP"))
        ALL_DMA = [sCON, sWA, sWB, sROW, sCC, sMISC] + sX + sW1 + sW2 + sOUT + sWQ
        sT = E(nc.semaphore("sT"))
        sV = E(nc.semaphore("sV"))
        sS = E(nc.semaphore("sS"))
        sP = E(nc.semaphore("sP"))

        def barrier():
            toks = [token(t.drain(), sT), token(v.drain(), sV),
                    token(s.drain(), sS), token(g.drain(), sP)]
            toks += [now(x) for x in ALL_DMA]
            for e in (t, v, s, g, sy):
                for tk in toks:
                    wait(e, tk)

        eyeb = E(nc.sbuf_tensor("eyeb_sb", [128, 128], BF16))
        eyef = E(nc.sbuf_tensor("eyef_sb", [128, 128], F32))
        onescb = E(nc.sbuf_tensor("onescb_sb", [128, 1], BF16))
        onesrf = E(nc.sbuf_tensor("onesrf_sb", [1, 128], F32))
        onesrb = E(nc.sbuf_tensor("onesrb_sb", [1, 128], BF16))
        bq_c = E(nc.sbuf_tensor("bq_c", [128, NF], F32))
        nbq_c = E(nc.sbuf_tensor("nbq_c", [128, NF], F32))
        b1_c = E(nc.sbuf_tensor("b1_c", [128, NQ], F32))
        pool = [E(nc.sbuf_tensor(f"pool{i}", [128, D], F32)) for i in range(4)]
        mu = {i: E(nc.sbuf_tensor(f"mu{i}", [128, NT], F32)) for i in range(4)}
        rr = {i: E(nc.sbuf_tensor(f"rr{i}", [128, NT], F32)) for i in range(4)}
        rs3 = E(nc.sbuf_tensor("rs3", [128, NT], F32))
        sc_sts = {i: E(nc.sbuf_tensor(f"sc_st{i}", [128, NT], F32)) for i in range(4)}
        sq_sts = {i: E(nc.sbuf_tensor(f"sq_st{i}", [128, NT], F32)) for i in range(4)}
        am_c = E(nc.sbuf_tensor("am_c", [128, NT], F32))
        smalls = E(nc.sbuf_tensor("smalls", [128, 8], F32))
        row128 = E(nc.sbuf_tensor("row128", [1, 128], F32))
        sc1 = E(nc.sbuf_tensor("sc1", [1, 8], F32))
        pad = E(nc.sbuf_tensor("pad", [1, 512], F32))
        sxB = E(nc.sbuf_tensor("sxB", [128, 2], F32))
        xslot = [E(nc.sbuf_tensor(f"xslot{i}", [128, D], F32)) for i in range(2)]
        rowf = E(nc.sbuf_tensor("rowf", [1, D], F32))
        epsln = E(nc.sbuf_tensor("epsln_sb", [128, 1], F32))
        b2_row = E(nc.sbuf_tensor("b2_row", [1, D], BF16))

        psB = [E(nc.psum_tensor(f"psB{i}", [128, 512], F32)) for i in range(4)]

        for dst, src in ((eyeb, deyeb), (eyef, deyef), (onescb, donescb)):
            token(g.dma_start(out=dst[:], in_=src[:, :]), sCON, 16)
        token(g.dma_start(out=onesrf[:1, :], in_=donesrf[:, :]), sCON, 16)
        token(g.dma_start(out=onesrb[:1, :], in_=donesrb[:, :]), sCON, 16)
        token(g.dma_start(out=epsln[:], in_=depsln[:, :]), sCON, 16)
        consts_tok = now(sCON)

        ones11 = onesrf[0:1, 0:1]
        rowf_free = None
        colld_prev = None

        def col_load(dst_cols, vec_ap, n_cols):
            nonlocal rowf_free, colld_prev
            etk = None
            for r in range((n_cols + 7) // 8):
                wait(sy, rowf_free)
                rtk = token(sy.dma_start(out=rowf[:1, :],
                                         in_=vec_ap[r * D:(r + 1) * D][None, :]), sROW, 16)
                wait(t, rtk)
                wait(t, consts_tok)
                wait(t, colld_prev)
                nb = min(8, n_cols - r * 8)
                for f in range(nb):
                    mm = t.matmul(psB[0][:, f:f + 1],
                                  rowf[0:1, f * 128:(f + 1) * 128], ones11,
                                  start=(f == 0), stop=(f == nb - 1))
                mtk = token(mm, sT)
                rowf_free = mtk
                wait(v, mtk)
                etk = token(v.tensor_copy(dst_cols[:, r * 8:r * 8 + nb],
                                          psB[0][:, 0:nb]), sV)
                colld_prev = etk
            return etk

        bqc_tok = col_load(bq_c, dvec["bq"], NF)
        b1c_tok = col_load(b1_c, db1, NQ)
        wait(v, bqc_tok)
        nbq_tok = token(v.tensor_scalar(nbq_c[:], bq_c[:], -1.0, None, op0=ALU.mult), sV)

        def ln_stats(i, c, src_sl, dep_v, dep_s):
            sc = sc_sts[i][:, c:c + 1]
            sq = sq_sts[i][:, c:c + 1]
            wait(v, dep_v)
            t1 = token(v.reduce_sum(sc, src_sl, axis=AX.X), sV)
            wait(s, dep_s)
            t2 = token(s.activation(pool[c % 2][:], src_sl, AF.Square,
                                    accum_out=sq), sS)
            wait(v, t1)
            t3 = token(v.tensor_scalar(mu[i][:, c:c + 1], sc,
                                       1.0 / D, None, op0=ALU.mult), sV)
            wait(v, t2)
            t4 = token(v.tensor_scalar(sq, sq, 1.0 / D, None, op0=ALU.mult), sV)
            wait(v, t3)
            t5 = token(v.tensor_tensor(sc, mu[i][:, c:c + 1],
                                       mu[i][:, c:c + 1], op=ALU.mult), sV)
            wait(v, t4)
            wait(v, t5)
            t6 = token(v.tensor_tensor(sq, sq, sc, op=ALU.subtract), sV)
            wait(s, t6)
            wait(s, consts_tok)
            t7 = token(s.activation(sq, sq, AF.Ln, bias=epsln[:, 0:1]), sS)
            wait(s, t7)
            t8 = token(s.activation(rr[i][:, c:c + 1], sq, AF.Exp,
                                    scale=-0.5), sS)
            return t8, t2

        ST_LT = ExitStack()
        ln0_tm = ST_LT.enter_context(nc.sbuf_tensor("ln0_tm", [128, NT * D], BF16))
        ST_W = ExitStack()
        WA = ST_W.enter_context(nc.sbuf_tensor("WA", [128, NF * D], BF16))
        WB = ST_W.enter_context(nc.sbuf_tensor("WB", [128, NF * D], BF16))
        ST_BR = ExitStack()
        bk_row = ST_BR.enter_context(nc.sbuf_tensor("bk_row", [1, D], BF16))
        bv_row = ST_BR.enter_context(nc.sbuf_tensor("bv_row", [1, D], BF16))
        bo_row = ST_BR.enter_context(nc.sbuf_tensor("bo_row", [1, D], BF16))
        ST_A = ExitStack()
        ln0_fm = ST_A.enter_context(nc.sbuf_tensor("ln0_fm", [128, NF * L], BF16))

        def load_small_w(dst, name, brow_dst, brow_name, sem, gate=None):
            if gate is not None:
                wait(g, gate)
            for k in range(NF):
                tk = token(g.dma_start(out=dst[:, k * D:(k + 1) * D],
                                       in_=wr[name][k]), sem, 16)
            if brow_dst is not None:
                tk = token(g.dma_start(out=brow_dst[:1, :],
                                       in_=dvec[brow_name][None, :]), sem, 16)
            return tk

        wv_tok = load_small_w(WA, "wv", bv_row, "bv", sWA)
        wk_tok = load_small_w(WB, "wk", bk_row, "bk", sWB)

        ln0_fm_r = ln0_fm[:].rearrange("p (f n) -> p f n", n=L)
        x_consumed = [None] * NT
        ln0fm_tok = [None] * NT
        ps_tr = [None, None]
        STP_A = ExitStack()
        ps_tr[0] = STP_A.enter_context(nc.psum_tensor("psTA0", [128, 1024], BF16))
        ps_tr[1] = STP_A.enter_context(nc.psum_tensor("psTA1", [128, 1024], BF16))
        tr_evac = [None] * NT

        def emit_A1(c):
            sl = xslot[c % 2]
            if c >= 2:
                wait(sy, x_consumed[c - 2])
            dtk = token(sy.dma_start(out=sl[:], in_=xr[c]), sX[c % 2], 16)
            return ln_stats(0, c, sl[:], dtk, dtk)[0]

        def emit_A2(c, r_tok):
            sl = xslot[c % 2]
            wait(v, r_tok)
            t9 = token(v.tensor_scalar(ln0_tm[:, c * D:(c + 1) * D], sl[:],
                                       mu[0][:, c:c + 1], rr[0][:, c:c + 1],
                                       op0=ALU.subtract, op1=ALU.mult), sV)
            x_consumed[c] = t9
            bank = ps_tr[c % 2]
            wait(t, t9)
            if c >= 2:
                wait(t, tr_evac[c - 2])
            for f in range(NF):
                mm = t.matmul(bank[:, f * 128:(f + 1) * 128],
                              ln0_tm[:, c * D + f * 128: c * D + (f + 1) * 128],
                              eyeb[:], is_transpose=True,
                              start=(f == 0), stop=(f == NF - 1))
            t10 = token(mm, sT)
            if c % 2 == 0:
                wait(s, t10)
                t11 = token(s.copy(ln0_fm_r[:, :, c * 128:(c + 1) * 128],
                                   bank[:].rearrange("p (f n) -> p f n", n=128)),
                            sS)
            else:
                wait(v, t10)
                t11 = token(v.tensor_copy(
                    ln0_fm_r[:, :, c * 128:(c + 1) * 128],
                    bank[:].rearrange("p (f n) -> p f n", n=128)), sV)
            ln0fm_tok[c] = t11
            tr_evac[c] = t11

        ST_VK = ExitStack()
        V_tm = ST_VK.enter_context(nc.sbuf_tensor("V_tm", [128, NT * D], BF16))
        K_tm = ST_VK.enter_context(nc.sbuf_tensor("K_tm", [128, NT * D], BF16))
        bank_free = [colld_prev, None, None, None]

        def proj_chunk(c, dst_tm, Wsl, brow, w_tok, elu, dst_tok, prev_elu):
            wait(t, ln0fm_tok[c])
            for j2 in range(2):
                b = (c * 2 + j2) % 4
                bank = psB[b]
                wait(t, bank_free[b])
                wait(t, w_tok)
                for k in range(NF):
                    t.matmul(bank[:],
                             ln0_fm[:, k * L + c * 128: k * L + (c + 1) * 128],
                             Wsl[:, k * D + j2 * 512: k * D + (j2 + 1) * 512],
                             start=(k == 0), stop=False)
                mm = t.matmul(bank[:], onesrb[:1, :],
                              brow[:1, j2 * 512:(j2 + 1) * 512],
                              start=False, stop=True)
                mtk = token(mm, sT)
                dst = dst_tm[:, c * D + j2 * 512: c * D + (j2 + 1) * 512]
                if not elu:
                    wait(v, mtk)
                    etk = token(v.tensor_copy(dst, bank[:]), sV)
                    bank_free[b] = etk
                    dst_tok[c] = etk
                else:
                    p1 = pool[2][:, j2 * 512:(j2 + 1) * 512]
                    p2 = pool[3][:, j2 * 512:(j2 + 1) * 512]
                    wait(s, mtk)
                    wait(s, prev_elu[j2])
                    a1 = token(s.activation(p1, bank[:], AF.Relu, scale=-1.0), sS)
                    wait(s, a1)
                    a2 = token(s.activation(p1, p1, AF.Exp, scale=-1.0), sS)
                    wait(v, mtk)
                    wait(v, prev_elu[j2])
                    wait(v, a1)
                    a3 = token(v.tensor_scalar(p2, bank[:], 0.0, None,
                                               op0=ALU.max), sV)
                    bank_free[b] = [a1, a3]
                    wait(v, a2)
                    wait(v, a3)
                    a4 = token(v.tensor_tensor(dst, p1, p2, op=ALU.add), sV)
                    dst_tok[c] = a4
                    prev_elu[j2] = a4

        v_tok = [None] * NT
        k_tok = [None] * NT
        rA = {}
        rA[0] = emit_A1(0)
        if NT > 1:
            rA[1] = emit_A1(1)
        for c in range(NT):
            emit_A2(c, rA[c])
            if c + 2 < NT:
                rA[c + 2] = emit_A1(c + 2)
            proj_chunk(c, V_tm, WA, bv_row, wv_tok, False, v_tok, None)
        wa_free = now(sT)

        for c in range(NT):
            wait(t, tr_evac[c])
        STP_A.close()

        prev_elu_k = [None, None]
        for c in range(NT):
            proj_chunk(c, K_tm, WB, bk_row, wk_tok, True, k_tok, prev_elu_k)
        wb_free = now(sT)

        ST_KS = ExitStack()
        KS = ST_KS.enter_context(nc.sbuf_tensor("KS", [128, NPAIR * 160], BF16, side="right"))
        STP_C = ExitStack()
        psC = [STP_C.enter_context(nc.psum_tensor(f"psC{i}", [128, 512], F32))
               for i in range(2)]
        ks_tok = [None] * NPAIR
        ksz_tok = token(v.memset(KS[:], 0.0), sV)
        wait(t, v_tok[NT - 1])
        wait(t, k_tok[NT - 1])
        for p in range(NPAIR):
            bank = psC[p % 2]
            if p >= 2:
                wait(t, ks_tok[p - 2])
            last_mm = None
            for hh in range(2):
                h = 2 * p + hh
                po, tp = (0, None) if hh == 0 else (64, (0, 64))
                for c in range(NT):
                    kl = K_tm[:, c * D + h * DH: c * D + (h + 1) * DH]
                    vl = V_tm[:, c * D + h * DH: c * D + (h + 1) * DH]
                    t.matmul(bank[po:po + 64, 0:64], kl, vl,
                             start=(c == 0), stop=False, tile_position=tp)
                    last_mm = t.matmul(bank[po:po + 64, 64:65], kl, onescb[:],
                                       start=False, stop=(c == NT - 1),
                                       tile_position=tp)
            mtk = token(last_mm, sT)
            wait(v, mtk)
            wait(v, ksz_tok)
            token(v.tensor_copy(KS[0:64, p * 160:p * 160 + 65],
                                bank[0:64, 0:65]), sV)
            ks_tok[p] = token(v.tensor_copy(KS[64:128, p * 160 + 80:p * 160 + 145],
                                            bank[64:128, 0:65]), sV)
        for p in range(NPAIR):
            wait(t, ks_tok[p])
        STP_C.close()
        ST_VK.close()

        ST_Q = ExitStack()
        Q_fm = ST_Q.enter_context(nc.sbuf_tensor("Q_fm", [128, NF * L], BF16, side="right"))
        wq_tok = load_small_w(WA, "wq", None, None, sWA, gate=wa_free)
        wo_tok = load_small_w(WB, "wo", bo_row, "bo", sWB, gate=wb_free)
        q_tok = [None] * (NF * NTT)
        qi = 0
        prev_elu = [None, None]
        for fo in range(NF):
            for tt in range(NTT):
                b = qi % 4
                bank = psB[b]
                wait(t, bank_free[b])
                wait(t, wq_tok)
                wait(t, ln0fm_tok[NT - 1])
                for k in range(NF):
                    mm = t.matmul(bank[:],
                                  WA[:, k * D + fo * 128:k * D + (fo + 1) * 128],
                                  ln0_fm[:, k * L + tt * 512: k * L + (tt + 1) * 512],
                                  start=(k == 0), stop=(k == NF - 1))
                mtk = token(mm, sT)
                p1 = pool[2][:, (qi % 2) * 512:(qi % 2) * 512 + 512]
                p2 = pool[3][:, (qi % 2) * 512:(qi % 2) * 512 + 512]
                wait(s, mtk)
                wait(s, prev_elu[qi % 2])
                a1 = token(s.activation(p1, bank[:], AF.Relu, scale=-1.0,
                                        bias=nbq_c[:, fo:fo + 1]), sS)
                wait(s, a1)
                a2 = token(s.activation(p1, p1, AF.Exp, scale=-1.0), sS)
                wait(v, mtk)
                wait(v, nbq_tok)
                wait(v, prev_elu[qi % 2])
                wait(v, a1)
                a3 = token(v.tensor_scalar(p2, bank[:], bq_c[:, fo:fo + 1], 0.0,
                                           op0=ALU.add, op1=ALU.max), sV)
                bank_free[b] = [a1, a3]
                wait(v, a2)
                wait(v, a3)
                a4 = token(v.tensor_tensor(
                    Q_fm[:, fo * L + tt * 512: fo * L + (tt + 1) * 512],
                    p1, p2, op=ALU.add), sV)
                q_tok[qi] = a4
                prev_elu[qi % 2] = a4
                qi += 1
        q_done_T = None
        ST_A.close()
        ln0fm_last_reader = now(sT)

        ST_AF = ExitStack()
        A_fm = ST_AF.enter_context(nc.sbuf_tensor("A_fm", [128, NF * L], BF16))
        Abig = [ST_AF.enter_context(nc.sbuf_tensor("Abig0", [128, 1024], BF16))] * 2
        zz = ST_AF.enter_context(nc.sbuf_tensor("zz", [128, 4 * H], F32))
        STP_E = ExitStack()
        psE = [STP_E.enter_context(nc.psum_tensor(f"psE{i}", [128, 512], F32))
               for i in range(2)]
        psEt = [STP_E.enter_context(nc.psum_tensor(f"psEt{i}", [128, 1024], BF16))
                for i in range(2)]
        A_fm_r = A_fm[:].rearrange("p (f n) -> p f n", n=L)
        a_fm_tok = [None] * NT
        zfree = [None, None]
        sfree = [None, None]
        abig_free = [None, None]
        tb_free = [None, None]
        wait(t, q_tok[NF * NTT - 1])
        for c in range(NT):
            par = c % 2
            zbank = psB[2 * par]
            sb = [psE[par], psB[2 * par + 1]]
            wait(t, bank_free[2 * par])
            wait(t, zfree[par])
            zmm = None
            for h in range(H):
                p, hh = h // 2, h % 2
                zmm = t.matmul(zbank[:, h:h + 1],
                               Q_fm[:, p * L + c * 128:p * L + (c + 1) * 128],
                               KS[:, p * 160 + 80 * hh + 64: p * 160 + 80 * hh + 65],
                               start=(h == 0), stop=(h == H - 1))
            zmtk = token(zmm, sT)
            bank_free[2 * par] = None
            wait(t, bank_free[2 * par + 1])
            wait(t, sfree[par])
            smm = [None, None]
            for h in range(H):
                p, hh = h // 2, h % 2
                smm[h // 8] = t.matmul(
                    sb[h // 8][:, (h % 8) * 64:(h % 8) * 64 + 64],
                    Q_fm[:, p * L + c * 128:p * L + (c + 1) * 128],
                    KS[:, p * 160 + 80 * hh: p * 160 + 80 * hh + 64],
                    start=(h % 8 == 0), stop=(h % 8 == 7))
            smtk = [token(smm[0], sT), token(smm[1], sT)]
            bank_free[2 * par + 1] = None
            zc = zz[:, 32 * par:32 * par + 32]
            wait(v, zmtk)
            zv1 = token(v.tensor_scalar(zc[:, 0:16], zbank[:, 0:16],
                                        EPS_ATTN, None, op0=ALU.add), sV)
            zfree[par] = zv1
            wait(v, zv1)
            zv2 = token(v.reciprocal(zc[:, 16:32], zc[:, 0:16]), sV)
            ab = Abig[par]
            wait(v, zv2)
            wait(v, abig_free[par])
            wait(s, zv2)
            wait(s, abig_free[par])
            sc_toks = []
            for h in range(H):
                src = sb[h // 8][:, (h % 8) * 64:(h % 8) * 64 + 64]
                dstA = ab[:, h * 64:(h + 1) * 64]
                if h // 8 == 0:
                    wait(v, smtk[0])
                    sc_toks.append(token(v.tensor_scalar(
                        dstA, src, zc[:, 16 + h:17 + h], None, op0=ALU.mult), sV))
                else:
                    wait(s, smtk[1])
                    sc_toks.append(token(s.activation(
                        dstA, src, AF.Copy, scale=zc[:, 16 + h:17 + h]), sS))
            sfree[par] = [sc_toks[7], sc_toks[15]]
            bank_t = psEt[par]
            wait(t, sc_toks[7])
            wait(t, sc_toks[15])
            wait(t, tb_free[par])
            tmm = None
            for p in range(NPAIR):
                tmm = t.matmul(bank_t[:, p * 128:(p + 1) * 128],
                               ab[:, p * 128:(p + 1) * 128],
                               eyeb[:], is_transpose=True,
                               start=(p == 0), stop=(p == NPAIR - 1))
            abig_free[par] = token(tmm, sT)
            wait(s, abig_free[par])
            etk = token(s.copy(A_fm_r[:, :, c * 128:(c + 1) * 128],
                               bank_t[:].rearrange("p (f n) -> p f n", n=128)), sS)
            tb_free[par] = etk
            a_fm_tok[c] = etk
        wait(t, tb_free[0])
        wait(t, tb_free[1])
        bank_free = [zfree[0], sfree[0][1], zfree[1], sfree[1][1]]
        STP_E.close()
        ST_Q.close()
        ST_KS.close()

        ST_U2 = ExitStack()
        u2_tm = ST_U2.enter_context(nc.sbuf_tensor("u2_tm", [128, NT * D], BF16, side="right"))
        ST_U1 = ExitStack()
        u1 = ST_U1.enter_context(nc.sbuf_tensor("u1", [128, NT * D], BF16, side="right"))
        ST_BC = ExitStack()
        g1B = ST_BC.enter_context(nc.sbuf_tensor("g1B", [128, D], BF16, side="right"))
        be1B = ST_BC.enter_context(nc.sbuf_tensor("be1B", [128, D], BF16, side="right"))
        u1_tok = [None] * NT
        row_prev = None

        def bcast_row_pre(dst, name):
            nonlocal row_prev
            wait(sy, row_prev)
            wait(sy, rowf_free)
            rtk = token(sy.dma_start(out=rowf[:1, :], in_=dvec[name][None, :]), sROW, 16)
            etk = None
            for j2 in range(2):
                bank = psB[j2]
                wait(t, rtk)
                wait(t, bank_free[j2])
                mm = t.matmul(bank[:], onesrf[:1, :],
                              rowf[:1, j2 * 512:(j2 + 1) * 512],
                              start=True, stop=True)
                mtk = token(mm, sT)
                wait(v, mtk)
                etk = token(v.tensor_copy(dst[:, j2 * 512:(j2 + 1) * 512],
                                          bank[:]), sV)
                bank_free[j2] = etk
            row_prev = etk
            return etk

        bcast_row_pre(g1B, "g1")
        g1b_tok = bcast_row_pre(be1B, "be1")
        token(g.dma_start(out=b2_row[:1, :], in_=dvec["b2"][None, :]), sMISC, 16)
        b2row_tok = now(sMISC)

        u2_tok = [None] * NT
        prev_ap = [None, None]
        for c in range(NT):
            wait(t, a_fm_tok[c])
            for j2 in range(2):
                b = (c * 2 + j2) % 4
                bank = psB[b]
                wait(t, bank_free[b])
                wait(t, wo_tok)
                for k in range(NF):
                    t.matmul(bank[:],
                             A_fm[:, k * L + c * 128: k * L + (c + 1) * 128],
                             WB[:, k * D + j2 * 512: k * D + (j2 + 1) * 512],
                             start=(k == 0), stop=False)
                mm = t.matmul(bank[:], onesrb[:1, :],
                              bo_row[:1, j2 * 512:(j2 + 1) * 512],
                              start=False, stop=True)
                mtk = token(mm, sT)
                wait(v, mtk)
                etk = token(v.tensor_tensor(
                    u1[:, c * D + j2 * 512: c * D + (j2 + 1) * 512],
                    bank[:],
                    ln0_tm[:, c * D + j2 * 512: c * D + (j2 + 1) * 512],
                    op=ALU.add), sV)
                bank_free[b] = etk
                u1_tok[c] = etk
            cd = slice(c * D, (c + 1) * D)
            r_tok, _ = ln_stats(1, c, u1[:, cd], u1_tok[c], u1_tok[c])
            po = pool[2 + (c % 2)]
            wait(v, r_tok)
            wait(v, prev_ap[c % 2])
            wait(v, g1b_tok)
            t9 = token(v.scalar_tensor_tensor(
                po[:], u1[:, cd], mu[1][:, c:c + 1], g1B[:],
                op0=ALU.subtract, op1=ALU.mult), sV)
            wait(v, t9)
            t11 = token(v.scalar_tensor_tensor(
                u2_tm[:, cd], po[:], rr[1][:, c:c + 1], be1B[:],
                op0=ALU.mult, op1=ALU.add), sV)
            u2_tok[c] = t11
            prev_ap[c % 2] = t11
        wo_last = now(sT)
        ST_AF.close()
        ST_BR.close()
        ST_W.close()
        ST_LT.close()

        ST_BC.close()
        ST_U1.close()

        g2B = ST_U2.enter_context(nc.sbuf_tensor("g2B", [128, D], BF16, side="right"))
        be2B = ST_U2.enter_context(nc.sbuf_tensor("be2B", [128, D], BF16, side="right"))
        fbB = ST_U2.enter_context(nc.sbuf_tensor("fbB", [128, D], F32, side="right"))
        bcast_row_pre(g2B, "g2")
        bcast_row_pre(be2B, "be2")
        bc_tok = bcast_row_pre(fbB, "fb")

        ST_WFQ = ExitStack()
        wfH = [ST_WFQ.enter_context(nc.sbuf_tensor(f"wfH{i}", [128, D], F32))
               for i in range(2)]
        wfa_h = [None, None]
        wf_comb = [None]

        def emit_wfamax(ks):
            for k in ks:
                ws = wfH[k % 2]
                wait(sy, wfa_h[k % 2])
                dtk = token(sy.dma_start(out=ws[:], in_=wr["wf"][k]), sW1[k % 2], 16)
                wait(v, dtk)
                wait(v, wf_comb[0])
                col = smalls[:, 4:5] if k == 0 else smalls[:, 5:6]
                ctk = token(v.reduce_max(col, ws[:], axis=AX.X,
                                         apply_absolute_value=True), sV)
                if k > 0:
                    wait(v, ctk)
                    ctk = token(v.tensor_tensor(smalls[:, 4:5], smalls[:, 4:5],
                                                smalls[:, 5:6], op=ALU.max), sV)
                wf_comb[0] = ctk
                wfa_h[k % 2] = ctk

        ST_W1 = ExitStack()
        W1res = ST_W1.enter_context(nc.sbuf_tensor("W1res", [128, NF * DFF], BF16))
        wait(g, wo_last)
        wait(g, u1_tok[NT - 1])
        w1_tok = None
        for k in range(NF):
            w1_tok = token(g.dma_start(out=W1res[:, k * DFF:(k + 1) * DFF],
                                       in_=w1r[k]), sWA, 16)

        barrier()

        ST_H = ExitStack()
        NW2 = 5
        W2buf = ST_H.enter_context(nc.sbuf_tensor("W2buf", [128, NW2 * D], BF16))
        u2fm = ST_H.enter_context(nc.sbuf_tensor("u2fm", [128, NF * 512], BF16))
        y1 = ST_H.enter_context(nc.sbuf_tensor("y1", [128, NQ * 512], BF16))
        h1t = [ST_H.enter_context(nc.sbuf_tensor(f"h1t{i}", [128, D], F32))
               for i in range(2)]
        u2fm_r = u2fm[:].rearrange("p (f n) -> p f n", n=512)

        h1_tok = [None] * NT
        am_tok = [None] * NT
        x2_consumed = [None] * NT
        w2_mm_tok = [None] * NQ
        psH = None
        w1buf_gate = [None, None]
        STP_T = None
        psum_evac_prev = None
        w2_mm_prev = None
        fm_done = [None]

        def emit_u2fm(tt2):
            STP_T2 = ExitStack()
            psT = STP_T2.enter_context(nc.psum_tensor(f"psTH{tt2}", [128, 1024], BF16))
            fm_last = None
            for ci in range(4):
                c = tt2 * 4 + ci
                wait(t, u2_tok[c])
                wait(t, fm_last)
                wait(t, psum_evac_prev)
                for f in range(NF):
                    mm = t.matmul(psT[:, f * 128:(f + 1) * 128],
                                  u2_tm[:, c * D + f * 128: c * D + (f + 1) * 128],
                                  eyeb[:], is_transpose=True,
                                  start=(f == 0), stop=(f == NF - 1))
                mtk = token(mm, sT)
                wait(s, mtk)
                fm_last = token(s.copy(
                    u2fm_r[:, :, ci * 128:(ci + 1) * 128],
                    psT[:].rearrange("p (f n) -> p f n", n=128)), sS)
            wait(t, fm_last)
            STP_T2.close()
            fm_done[0] = fm_last

        def emit_W1(tt):
            y1_tok = [None] * NQ
            wait(t, w1_tok)
            for i in range(NQ):
                b = i % 4
                bank = psB[b]
                wait(t, bank_free[b])
                for k in range(NF):
                    mm = t.matmul(
                        bank[:],
                        W1res[:, k * DFF + i * 128: k * DFF + (i + 1) * 128],
                        u2fm[:, k * 512:(k + 1) * 512],
                        start=(k == 0), stop=(k == NF - 1))
                mtk = token(mm, sT)
                if i % 2 == 0:
                    wait(v, mtk)
                    etk = token(v.tensor_scalar(
                        y1[:, i * 512:(i + 1) * 512], bank[:],
                        b1_c[:, i:i + 1], 0.0, op0=ALU.add, op1=ALU.max), sV)
                else:
                    wait(s, mtk)
                    etk = token(s.activation(
                        y1[:, i * 512:(i + 1) * 512], bank[:],
                        AF.Relu, bias=b1_c[:, i:i + 1]), sS)
                bank_free[b] = etk
                y1_tok[i] = etk
            return y1_tok

        def emit_W2(tt, y1_tok):
            nonlocal psum_evac_prev, w2_mm_prev
            STP_H = ExitStack()
            psH = [STP_H.enter_context(nc.psum_tensor(f"psH{tt}_{i}", [128, 512], F32))
                   for i in range(4)]
            psW2 = psB + psH
            for b in range(4):
                wait(t, bank_free[b])
                bank_free[b] = None
            last_mm = {}
            for k2 in range(NQ):
                sl = k2 % NW2
                if k2 >= NW2:
                    wait(g, w2_mm_tok[k2 - NW2])
                elif w2_mm_prev is not None:
                    wait(g, w2_mm_prev)
                dtk = token(g.dma_start(out=W2buf[:, sl * D:(sl + 1) * D],
                                        in_=w2r[k2]), sW2[sl], 16)
                wait(t, dtk)
                wait(t, y1_tok[k2])
                for c2 in range(4):
                    for j2 in range(2):
                        bank = psW2[c2 * 2 + j2]
                        last_mm[(c2, j2)] = t.matmul(
                            bank[:],
                            y1[:, k2 * 512 + c2 * 128: k2 * 512 + (c2 + 1) * 128],
                            W2buf[:, sl * D + j2 * 512: sl * D + (j2 + 1) * 512],
                            start=(k2 == 0), stop=False)
                w2_mm_tok[k2] = token(last_mm[(3, 1)], sT)
            ev_toks = {}
            wait(t, b2row_tok)
            for c2 in range(4):
                for j2 in range(2):
                    bank = psW2[c2 * 2 + j2]
                    mm = t.matmul(bank[:], onesrb[:1, :],
                                  b2_row[:1, j2 * 512:(j2 + 1) * 512],
                                  start=False, stop=True)
                    mtk = token(mm, sT)
                    c = tt * 4 + c2
                    wait(v, mtk)
                    etk = token(v.tensor_tensor(
                        u2_tm[:, c * D + j2 * 512: c * D + (j2 + 1) * 512],
                        bank[:],
                        u2_tm[:, c * D + j2 * 512: c * D + (j2 + 1) * 512],
                        op=ALU.add), sV)
                    ev_toks[(c2, j2)] = etk
            psum_evac_prev = ev_toks[(3, 1)]
            w2_mm_prev = w2_mm_tok[NQ - 1]
            for b in range(4):
                bank_free[b] = ev_toks[(b // 2, b % 2)]
            STP_H.close()
            return ev_toks

        def emit_LN2(tt, ev_toks):
            prev_ap2 = [None]
            prev_am = [None]
            st1 = {}

            def ln2_part1(c2):
                c = tt * 4 + c2
                cd = slice(c * D, (c + 1) * D)
                dep = ev_toks[(c2, 1)]
                wait(v, ev_toks[(c2, 0)])
                wait(s, ev_toks[(c2, 0)])
                r_tok, _ = ln_stats(2, c, u2_tm[:, cd], dep, dep)
                wait(v, r_tok)
                wait(v, prev_ap2[0])
                t9 = token(v.scalar_tensor_tensor(
                    pool[2][:], u2_tm[:, cd], mu[2][:, c:c + 1], g2B[:],
                    op0=ALU.subtract, op1=ALU.mult), sV)
                sl = xslot[c % 2]
                if c >= 2:
                    wait(sy, x2_consumed[c - 2])
                else:
                    wait(sy, x_consumed[NT - 1])
                dtk = token(sy.dma_start(out=sl[:], in_=xr[c]), sX[c % 2], 16)
                if c >= 2:
                    wait(v, x2_consumed[c - 2])
                wait(v, t9)
                t11 = token(v.scalar_tensor_tensor(
                    u2_tm[:, cd], pool[2][:], rr[2][:, c:c + 1], be2B[:],
                    op0=ALU.mult, op1=ALU.add), sV)
                prev_ap2[0] = t11
                wait(v, dtk)
                wait(v, t11)
                t12 = token(v.tensor_tensor(h1t[c % 2][:], u2_tm[:, cd], sl[:],
                                            op=ALU.add), sV)
                h1_tok[c] = t12
                r3_tok, _ = ln_stats(3, c, h1t[c % 2][:], t12, t12)
                st1[c2] = (c, t12, r3_tok)

            def ln2_part2(c2):
                c, t12, r3_tok = st1[c2]
                wait(s, r3_tok)
                wait(s, prev_am[0])
                t21 = token(s.activation(pool[3][:], h1t[c % 2][:], AF.Identity,
                                         bias=mu[3][:, c:c + 1], scale=-1.0), sS)
                wait(v, t21)
                t22 = token(v.reduce_max(smalls[:, 2:3], pool[3][:], axis=AX.X,
                                         apply_absolute_value=True), sV)
                wait(v, t22)
                t23 = token(v.tensor_tensor(am_c[:, c:c + 1], smalls[:, 2:3],
                                            rr[3][:, c:c + 1], op=ALU.mult), sV)
                am_tok[c] = t23
                prev_am[0] = [t23, t21]
                x2_consumed[c] = [t12, t23, t21, r3_tok]

            for c2 in range(4):
                ln2_part1(c2)
                if c2 >= 1:
                    ln2_part2(c2 - 1)
            ln2_part2(3)

        emit_u2fm(0)
        y1_cur = emit_W1(0)
        for tt in range(NTT):
            ev_cur = emit_W2(tt, y1_cur)
            if tt + 1 < NTT:
                emit_u2fm(tt + 1)
                y1_cur = emit_W1(tt + 1)
            emit_LN2(tt, ev_cur)
            wfp = -(-NF // max(1, min(NTT, 3)))
            emit_wfamax(list(range(NF))[tt * wfp:(tt + 1) * wfp])

        wq_dtk = [None] * NF
        for k in range(2):
            wait(sy, wfa_h[k % 2])
            wq_dtk[k] = token(sy.dma_start(out=wfH[k][:], in_=wr["wf"][k]),
                              sW1[k % 2], 16)

        wait(sy, now(sT))
        wait(sy, now(sV))
        wait(sy, now(sS))
        ST_H.close()
        ST_W1.close()

        ST_I = ExitStack()
        wfq = ST_I.enter_context(nc.sbuf_tensor("wfq", [128, NF * D], BF16))
        wait(v, am_tok[NT - 1])
        t1 = token(v.reduce_max(smalls[:, 3:4], am_c[:, 0:NT], axis=AX.X), sV)
        wait(t, t1)
        mm = t.matmul(psB[0][:1, 0:128], smalls[:, 3:4], eyef[:], start=True, stop=True)
        t2 = token(mm, sT)
        wait(v, t2)
        t3 = token(v.tensor_copy(row128[:1, :], psB[0][:1, 0:128]), sV)
        wait(v, t3)
        t4 = token(v.reduce_max(sc1[:1, 0:1], row128[:1, :], axis=AX.X), sV)
        t5 = token(v.memset(pad[:1, :], 0.0), sV)
        wait(v, t4)
        wait(v, t5)
        t6 = token(v.tensor_copy(pad[:1, 0:1], sc1[:1, 0:1]), sV)
        wait(g, t6)
        t7 = token(g.dma_start(out=cc_in[:, :], in_=pad[:1, :]), sCC, 16)
        wait(g, t7)
        cctk = token(g.collective_compute(
            "AllReduce", ALU.max,
            replica_groups=[list(range(N_CORES))],
            ins=[cc_in[:, :]], outs=[cc_out[:, :]]), sCCP)

        wait(t, wf_comb[0])
        mm = t.matmul(psB[1][:1, 0:128], smalls[:, 4:5], eyef[:], start=True, stop=True)
        t9 = token(mm, sT)
        wait(v, t9)
        t10 = token(v.tensor_copy(row128[:1, :], psB[1][:1, 0:128]), sV)
        wait(v, t10)
        t11 = token(v.reduce_max(sc1[:1, 1:2], row128[:1, :], axis=AX.X), sV)
        wait(v, t11)
        t13 = token(v.tensor_scalar(sc1[:1, 1:2], sc1[:1, 1:2], 1e-12, None,
                                    op0=ALU.max), sV)
        wait(v, t13)
        t14a = token(v.reciprocal(sc1[:1, 2:3], sc1[:1, 1:2]), sV)
        wait(v, t14a)
        t14 = token(v.tensor_scalar(sc1[:1, 2:3], sc1[:1, 2:3], 448.0, None,
                                    op0=ALU.mult), sV)
        t15 = token(v.tensor_scalar(sc1[:1, 3:4], sc1[:1, 1:2], 1.0 / 448.0, None,
                                    op0=ALU.mult), sV)
        wait(t, t14)
        mm = t.matmul(psB[2][:, 0:1], onesrf[:1, :], sc1[:1, 2:3], start=True, stop=True)
        t16 = token(mm, sT)
        wait(v, t16)
        t17 = token(v.tensor_copy(smalls[:, 6:7], psB[2][:, 0:1]), sV)
        wsI = [ST_I.enter_context(nc.sbuf_tensor(f"wsI{i}", [128, D], F32))
               for i in range(2)]
        wslots = [wfH[0], wfH[1], wsI[0], wsI[1]]
        wsem = [sW1[0], sW1[1], sWQ[0], sWQ[1]]
        jwfq_tok = [None] * NF
        wfa = [None] * 4
        qv_prev = None
        qg_prev = None
        for k in range(NF):
            ws = wslots[k % 4]
            if wq_dtk[k] is None:
                wait(sy, wfa[k % 4])
                wq_dtk[k] = token(sy.dma_start(out=ws[:], in_=wr["wf"][k]),
                                  wsem[k % 4], 16)
            wait(v, wq_dtk[k])
            wait(v, t17)
            wait(v, qg_prev)
            q1 = token(v.tensor_scalar(pool[0][:], ws[:], smalls[:, 6:7], None,
                                       op0=ALU.mult), sV)
            wfa[k % 4] = q1
            wait(s, q1)
            wait(s, qv_prev)
            q2 = token(s.mul(pool[1][:], pool[0][:], CV), sS)
            wait(g, q2)
            wait(g, qv_prev)
            q3 = token(g.tensor_tensor(pool[2][:], pool[1][:], pool[0][:],
                                       op=ALU.subtract), sP)
            qg_prev = q3
            wait(v, q3)
            q4 = token(v.tensor_tensor(wfq[:, k * D:(k + 1) * D], pool[1][:],
                                       pool[2][:], op=ALU.subtract), sV)
            qv_prev = q4
            jwfq_tok[k] = q4
        wfq_tok = now(sV)

        wait(g, cctk)
        ccb = token(g.dma_start(out=pad[:1, :], in_=cc_out[:, :]), sCC, 16)
        wait(v, ccb)
        t19 = token(v.tensor_scalar(sc1[:1, 4:5], pad[:1, 0:1], 1e-12, None,
                                    op0=ALU.max), sV)
        wait(v, t19)
        t20a = token(v.reciprocal(sc1[:1, 5:6], sc1[:1, 4:5]), sV)
        wait(v, t20a)
        t20 = token(v.tensor_scalar(sc1[:1, 5:6], sc1[:1, 5:6], 448.0, None,
                                    op0=ALU.mult), sV)
        t21 = token(v.tensor_scalar(sc1[:1, 6:7], sc1[:1, 4:5], 1.0 / 448.0, None,
                                    op0=ALU.mult), sV)
        wait(v, t21)
        t22 = token(v.tensor_tensor(sc1[:1, 7:8], sc1[:1, 6:7], sc1[:1, 3:4],
                                    op=ALU.mult), sV)
        wait(t, t20)
        mm = t.matmul(psB[3][:, 0:1], onesrf[:1, :], sc1[:1, 5:6], start=True, stop=True)
        token(mm, sT)
        wait(t, t22)
        mm2 = t.matmul(psB[3][:, 1:2], onesrf[:1, :], sc1[:1, 7:8], start=True, stop=True)
        t23 = token(mm2, sT)
        wait(v, t23)
        t24 = token(v.tensor_copy(sxB[:], psB[3][:, 0:2]), sV)
        wait(v, t24)
        rs_tok = token(v.tensor_scalar(rs3[:, 0:NT], rr[3][:, 0:NT],
                                       sxB[:, 0:1], None, op0=ALU.mult), sV)

        xq_fm = ST_I.enter_context(nc.sbuf_tensor("xq_fm", [128, NF * L], BF16))
        qtm = [ST_I.enter_context(nc.sbuf_tensor(f"qtm{i}", [128, D], BF16))
               for i in range(4)]
        outst = [ST_I.enter_context(nc.sbuf_tensor(f"outst{i}", [128, D], F32))
                 for i in range(2)]
        h1j = [ST_I.enter_context(nc.sbuf_tensor(f"h1j{i}", [128, D], F32))
               for i in range(4)]
        STP_J = ExitStack()
        psJ = [STP_J.enter_context(nc.psum_tensor(f"psJ{i}", [128, 1024], BF16))
               for i in range(2)]
        xqfm_tok = [None] * NT
        qtm_free = [None] * 4
        jq_prev = [None, None]
        jtr_prev = [None, None]
        store_tok = [None] * NT
        ost_free = [None, None]
        hj_free = [None] * 4
        xsl_free = [None, None]
        hb_tok = [None] * NT
        jq4 = [None] * NT

        def emit_chain(c):
            cd = slice(c * D, (c + 1) * D)
            if c >= 2:
                wait(sy, xsl_free[c % 2])
            else:
                wait(sy, x2_consumed[NT - 2 + c])
            dtk = token(sy.dma_start(out=xslot[c % 2][:], in_=xr[c]), sX[c % 2], 16)
            wait(v, dtk)
            wait(v, hj_free[c % 4] if c >= 4 else None)
            hj = token(v.tensor_tensor(h1j[c % 4][:], u2_tm[:, cd],
                                       xslot[c % 2][:], op=ALU.add), sV)
            xsl_free[c % 2] = hj
            wait(v, jq_prev[c % 2])
            wait(v, rs_tok)
            wait(v, hj)
            pa = pool[2 * (c % 2)]
            pb = pool[2 * (c % 2) + 1]
            q1 = token(v.tensor_scalar(pa[:], h1j[c % 4][:],
                                       mu[3][:, c:c + 1], rs3[:, c:c + 1],
                                       op0=ALU.subtract, op1=ALU.mult), sV)
            wait(s, q1)
            q2 = token(s.mul(pb[:], pa[:], CV), sS)
            wait(g, q2)
            q3 = token(g.tensor_tensor(pa[:], pb[:], pa[:],
                                       op=ALU.subtract), sP)
            wait(v, qtm_free[c % 4])
            wait(v, q3)
            q4 = token(v.tensor_tensor(qtm[c % 4][:], pb[:], pa[:],
                                       op=ALU.subtract), sV)
            jq_prev[c % 2] = q4
            jq4[c] = q4
            wait(g, bc_tok)
            wait(g, q1)
            hb_tok[c] = token(g.tensor_tensor(h1j[c % 4][:], h1j[c % 4][:],
                                              fbB[:], op=ALU.add), sP)

        def emit_tr(c):
            wait(t, jq4[c])
            wait(t, jtr_prev[c % 2])
            for f in range(NF):
                mm = t.matmul(psJ[c % 2][:, f * 128:(f + 1) * 128],
                              qtm[c % 4][:, f * 128:(f + 1) * 128],
                              eyeb[:], is_transpose=True,
                              start=(f == 0), stop=(f == NF - 1))
            t10 = token(mm, sT)
            wait(s, t10)
            t11 = token(s.copy(xq_fm[:, c * D:(c + 1) * D], psJ[c % 2][:]), sS)
            xqfm_tok[c] = t11
            qtm_free[c % 4] = t11
            jtr_prev[c % 2] = t11

        def emit_jmm(c):
            t11 = xqfm_tok[c]
            ost = outst[c % 2]
            ev2 = None
            for j2 in range(2):
                b = (c * 2 + j2) % 4
                bank = psB[b]
                wait(t, bank_free[b])
                wait(t, t11)
                for k in range(NF):
                    wait(t, jwfq_tok[k])
                    mm = t.matmul(bank[:],
                                  xq_fm[:, c * D + k * 128: c * D + (k + 1) * 128],
                                  wfq[:, k * D + j2 * 512:k * D + (j2 + 1) * 512],
                                  start=(k == 0), stop=(k == NF - 1))
                mtk = token(mm, sT)
                wait(v, mtk)
                wait(v, hb_tok[c])
                wait(v, ost_free[c % 2])
                ev2 = token(v.scalar_tensor_tensor(
                    ost[:, j2 * 512:(j2 + 1) * 512], bank[:],
                    sxB[:, 1:2], h1j[c % 4][:, j2 * 512:(j2 + 1) * 512],
                    op0=ALU.mult, op1=ALU.add), sV)
                bank_free[b] = ev2
            hj_free[c % 4] = ev2
            wait(g, ev2)
            store_tok[c] = token(g.dma_start(out=yr[c], in_=ost[:]), sOUT[c % 2], 16)
            ost_free[c % 2] = store_tok[c]

        emit_chain(0)
        emit_chain(1)
        emit_chain(2)
        emit_tr(0)
        for c in range(NT):
            if c + 3 < NT:
                emit_chain(c + 3)
            if c + 1 < NT:
                emit_tr(c + 1)
            emit_jmm(c)

        wait(g, now(sOUT[0]))
        wait(g, now(sOUT[1]))
        STP_J.close()
        ST_I.close()
        ST_WFQ.close()
        ST_U2.close()

    return nc


_CACHE = {}


def _consts():
    import ml_dtypes
    eye_f = np.eye(128, dtype=np.float32)
    return {
        "eyeb": eye_f.astype(ml_dtypes.bfloat16),
        "eyef": eye_f,
        "onescb": np.ones((128, 1), dtype=ml_dtypes.bfloat16),
        "onesrf": np.ones((1, 128), dtype=np.float32),
        "onesrb": np.ones((1, 128), dtype=ml_dtypes.bfloat16),
    }


def make_in_maps(inputs):
    x = np.ascontiguousarray(np.asarray(inputs["L0_attn_x"], dtype=np.float32))
    N = x.shape[0]
    base = {
        "wq": np.ascontiguousarray(inputs["L0_attn_Wq"], dtype=np.float32),
        "wk": np.ascontiguousarray(inputs["L0_attn_Wk"], dtype=np.float32),
        "wv": np.ascontiguousarray(inputs["L0_attn_Wv"], dtype=np.float32),
        "wo": np.ascontiguousarray(inputs["L0_attn_Wo"], dtype=np.float32),
        "w1": np.ascontiguousarray(inputs["L0_attn_W1"], dtype=np.float32),
        "w2": np.ascontiguousarray(inputs["L0_attn_W2"], dtype=np.float32),
        "wf": np.ascontiguousarray(np.asarray(inputs["L0_ffn_weight"], dtype=np.float32).T),
        "bq": np.ascontiguousarray(inputs["L0_attn_bq"], dtype=np.float32),
        "bk": np.ascontiguousarray(inputs["L0_attn_bk"], dtype=np.float32),
        "bv": np.ascontiguousarray(inputs["L0_attn_bv"], dtype=np.float32),
        "bo": np.ascontiguousarray(inputs["L0_attn_bo"], dtype=np.float32),
        "b1": np.ascontiguousarray(inputs["L0_attn_b1"], dtype=np.float32),
        "b2": np.ascontiguousarray(inputs["L0_attn_b2"], dtype=np.float32),
        "fb": np.ascontiguousarray(inputs["L0_ffn_bias"], dtype=np.float32),
        "g1": np.ascontiguousarray(inputs["L0_attn_g1"], dtype=np.float32),
        "be1": np.ascontiguousarray(inputs["L0_attn_beta1"], dtype=np.float32),
        "g2": np.ascontiguousarray(inputs["L0_attn_g2"], dtype=np.float32),
        "be2": np.ascontiguousarray(inputs["L0_attn_beta2"], dtype=np.float32),
        "epsln": np.full((128, 1), EPS_LN, dtype=np.float32),
        **_consts(),
    }
    return [dict(base, x=np.ascontiguousarray(x[n])) for n in range(N)]


def kernel(**inputs):
    x = np.asarray(inputs["L0_attn_x"])
    N, L, Dm = x.shape
    assert Dm == D and N == N_CORES
    if L not in _CACHE:
        _CACHE[L] = build(L)
    nc = _CACHE[L]
    in_maps = make_in_maps(inputs)
    res = run_bass_kernel_spmd(nc, in_maps, core_ids=list(range(N_CORES)))
    return np.stack([np.asarray(r["y"]) for r in res.results], axis=0)
